# version 05
# baseline (speedup 1.0000x reference)
import contextlib
import numpy as np
import ml_dtypes
import concourse.bass as bass
import concourse.mybir as mybir
from concourse.bass_utils import run_bass_kernel_spmd

F32 = mybir.dt.float32
BF16 = mybir.dt.bfloat16
AF = mybir.ActivationFunctionType
ALU = mybir.AluOpType
BF = ml_dtypes.bfloat16

T = 2048
QT = 4
QS = 512
CH = 128
DC = 8
FC = 32
H = 8
EPS = 1e-5
SHUF = [i ^ 1 for i in range(32)]

_PAR = {}
_c = 0
for _nm, _n in [("ln1w", 16), ("ln1b", 16), ("ln2w", 16), ("ln2b", 16),
                ("gnw", 16), ("gnb", 16), ("fb1", 64), ("fb2", 16),
                ("w1s", 32), ("w2s", 32), ("h1s", 32), ("h2s", 32),
                ("w3s", 8)]:
    _PAR[_nm] = _c
    _c += _n
NPAR = _c

CON_MASK = 0
CON_O128 = 128
CON_O1024 = 256
CON_GAM = 384
CON_EPS = 384 + H * 128
CON_NH = CON_EPS + 1
NCON = CON_EPS + 2


def _toks(waits):
    return [w for w in waits if w is not None]


class B:

    ENGS = ("tensor", "vector", "scalar", "sync", "gpsimd")
    DQ = {"dS": "sync", "dG": "gpsimd", "dA": "scalar"}
    OWNER = {"sync": "dS", "gpsimd": "dG", "scalar": "dA"}

    def __init__(self, nc, sems):
        self.nc, self.sems = nc, sems
        self.val = {k: 0 for k in sems}
        self.hw = {}
        self.mode = None
        self.e = None
        self.scout = False
        self.groups = {s: [0] for s in ("dS", "dG", "dA")}

    def _flush(self):
        for sem, eng in self.DQ.items():
            self.wait(eng, self.tok(sem))

    def run_phase(self, emit):
        ev, eh = dict(self.val), dict(self.hw)
        self.scout, self.mode, self.e = True, None, None
        self.val, self.hw = dict(ev), dict(eh)
        emit(self)
        self._flush()
        self.scout = False
        ret = [None]
        with self.nc.Block() as blk:
            for nm in self.ENGS:
                def body(e, nm=nm):
                    self.val, self.hw = dict(ev), dict(eh)
                    self.mode, self.e = nm, e
                    ret[0] = emit(self)
                    self._flush()
                getattr(blk, nm)(body)
        self.mode, self.e = None, None
        return ret[0]

    def _inc(self, ins, sem, amt):
        if ins is not None:
            ins.then_inc(self.sems[sem], amt)
        self.val[sem] += amt
        return (sem, self.val[sem])

    def tok(self, sem):
        return (sem, self.val[sem])

    def _fix(self, sem, v):
        g = self.groups[sem]
        import bisect
        i = bisect.bisect_left(g, v)
        if i == len(g):
            raise RuntimeError(f"no group end >= {v} for {sem} (have {g[-5:]})")
        return g[i]

    def wait(self, engname, *tokens):
        for t in tokens:
            if t is None:
                continue
            sem, v0 = t
            if v0 <= 0:
                continue
            key = (engname, sem)
            if self.hw.get(key, 0) >= v0:
                continue
            own = self.OWNER.get(engname)
            if self.scout:
                if own is not None:
                    g = self.groups[own]
                    if g[-1] < self.val[own]:
                        g.append(self.val[own])
            else:
                if own is not None:
                    skey = (engname, "_self")
                    if self.val[own] > self.hw.get(skey, 0):
                        if self.mode == engname:
                            self.e.wait_ge(self.sems[own], self.val[own])
                        self.hw[skey] = self.val[own]
                if self.mode == engname:
                    v = self._fix(sem, v0) if sem in self.groups else v0
                    self.e.wait_ge(self.sems[sem], v)
            self.hw[key] = v0

    def snapc(self):
        return [self.tok(s) for s in ("T", "V", "A", "dA", "dG", "dS")]

    def dma(self, q, out, in_, waits=()):
        ename = {"S": "sync", "G": "gpsimd", "A": "scalar"}[q]
        sem = {"S": "dS", "G": "dG", "A": "dA"}[q]
        self.wait(ename, *_toks(waits))
        ins = (self.e.dma_start(out=out, in_=in_)
               if self.mode == ename else None)
        return self._inc(ins, sem, 16)

    def mm(self, out, lhsT, rhs, start, stop, waits=(), sig=False):
        self.wait("tensor", *_toks(waits))
        ins = (self.e.matmul(out, lhsT, rhs, start=start, stop=stop)
               if self.mode == "tensor" else None)
        return self._inc(ins, "T", 1) if sig else None

    def tr(self, out, in_, ident, waits=(), sig=False):
        self.wait("tensor", *_toks(waits))
        ins = (self.e.transpose(out, in_, ident)
               if self.mode == "tensor" else None)
        return self._inc(ins, "T", 1) if sig else None

    def v(self, name, *args, waits=(), sig=False, **kw):
        self.wait("vector", *_toks(waits))
        ins = None
        if self.mode == "vector":
            self.e.drain()
            ins = getattr(self.e, name)(*args, **kw)
        return self._inc(ins, "V", 1) if sig else None

    def a(self, name, *args, waits=(), sig=False, **kw):
        self.wait("scalar", *_toks(waits))
        ins = None
        if self.mode == "scalar":
            if name != "dma_start":
                self.e.drain()
            ins = getattr(self.e, name)(*args, **kw)
        return self._inc(ins, "A", 1) if sig else None

    def collective(self, kind, op, replica_groups, ins_, outs_, waits=()):
        self.wait("gpsimd", *_toks(waits))
        ins = (self.e.collective_compute(kind, op,
                                         replica_groups=replica_groups,
                                         ins=ins_, outs=outs_)
               if self.mode == "gpsimd" else None)
        return self._inc(ins, "C", 1)

    def split(self, hi, lo, src, waits=(), sig=True):
        self.v("tensor_copy", hi, src, waits=waits)
        return self.v("tensor_tensor", lo, src, hi, ALU.subtract, sig=sig)


def mm3(bb, out, wpair, apair, start, stop, waits=(), sig=False):
    whi, wlo = wpair
    ahi, alo = apair
    bb.mm(out, whi, ahi, start=start, stop=False, waits=waits)
    bb.mm(out, whi, alo, start=False, stop=False)
    return bb.mm(out, wlo, ahi, start=False, stop=stop, sig=sig)


def Q(i):
    return slice(i * QS, (i + 1) * QS)


def C1(i):
    return slice(i * 128, (i + 1) * 128)


def build(debug=False, stop=None):
    nc = bass.Bass("TRN2", target_bir_lowering=False,
                   detect_race_conditions=False)

    def din(n, s, d=F32):
        return nc.dram_tensor(n, s, d, kind="ExternalInput")

    def dscr(n, s, d=F32):
        if debug:
            return nc.dram_tensor(n, s, d, kind="ExternalOutput")
        return nc.dram_tensor(n, s, d)

    D = {}
    D["x0"] = din("x0", [QT, 128, DC * QS])
    D["wq"] = din("wq", [2, H, 128, DC * 128])
    D["wk"] = din("wk", [2, H, 128, DC * 128])
    D["wv"] = din("wv", [2, H, 128, DC * 128])
    D["wg"] = din("wg", [2, H, 128, DC * 128])
    D["wo"] = din("wo", [2, DC, 128, DC * 128])
    D["fw1"] = din("fw1", [2, FC, 128, DC * 128])
    D["fw2"] = din("fw2", [2, DC, 128, FC * 128])
    D["tw1"] = din("tw1", [FC, 128, DC * 128])
    D["tw2"] = din("tw2", [FC, 128, DC * 128])
    D["th1"] = din("th1", [FC, 128, FC * 128])
    D["th2"] = din("th2", [FC, 128, FC * 128])
    D["tw3"] = din("tw3", [DC, 128, FC * 128])
    D["xtab"] = din("xtab", [H, QT, 128, 4 * QS])
    D["pars"] = din("pars", [128, NPAR])
    D["cons"] = din("cons", [128, NCON])
    D["consb"] = din("consb", [128, 128], BF16)
    D["maska"] = din("maska", [128, 1])
    D["out"] = nc.dram_tensor("out", [QT, 128, DC * QS], F32,
                              kind="ExternalOutput")
    D["sc_y"] = dscr("sc_y", [QT, 128, DC * QS])
    D["sc_x"] = dscr("sc_x", [QT, 128, DC * QS])
    D["sc_x1"] = dscr("sc_x1", [QT, 128, DC * QS])
    D["sc_ln1"] = dscr("sc_ln1", [QT, 2, 128, DC * QS], BF16)
    D["sc_kv"] = dscr("sc_kv", [H, QT, 6, 128, QS], BF16)
    D["sc_w1"] = dscr("sc_w1", [2, FC, 2, 128, DC * 128], BF16)
    D["sc_w2"] = dscr("sc_w2", [2, DC, 2, 128, FC * 128], BF16)
    D["cc_in"] = dscr("cc_in", [128, H * 128])
    D["cc_out"] = dscr("cc_out", [256, H * 128])

    with contextlib.ExitStack() as ST:
        sems = {n: ST.enter_context(nc.semaphore(n))
                for n in ("dS", "dG", "dA", "T", "V", "A", "C")}
        bb = B(nc, sems)

        sbp = lambda n, s, d=F32: ST.enter_context(nc.sbuf_tensor(n, s, d))
        G = {}
        G["c_par"] = sbp("c_par", [128, NPAR])
        G["c_con"] = sbp("c_con", [128, NCON])
        G["c_idb"] = sbp("c_idb", [128, 128], BF16)
        G["c_ma"] = sbp("c_ma", [128, 1])
        G["sq"] = sbp("sq", [128, 2 * QS])
        G["mu"] = sbp("mu", [128, QS])
        G["msq"] = sbp("msq", [128, QS])
        G["var"] = sbp("var", [128, QS])
        G["rstd"] = sbp("rstd", [128, QS])
        G["lnf"] = sbp("lnf", [128, QS])
        G["pb"] = [ST.enter_context(nc.psum_tensor(f"pb{i}", [128, 512], F32))
                   for i in range(7)]
        G["pt"] = ST.enter_context(nc.psum_tensor("pt", [128, 1024], BF16))

        cc = G["c_con"]
        G["MASK"] = cc[:, CON_MASK:CON_MASK + 128]
        G["O128"] = cc[:, CON_O128:CON_O128 + 128]
        G["O1024"] = cc[:, CON_O1024:CON_O1024 + 128]
        G["GAM"] = lambda h: cc[:, CON_GAM + h * 128:CON_GAM + (h + 1) * 128]
        G["P1"] = lambda nm, i: G["c_par"][:, _PAR[nm] + i:_PAR[nm] + i + 1]
        G["IDB"] = G["c_idb"][:, :]
        G["EPSA"] = cc[:, CON_EPS:CON_EPS + 1]
        G["NHA"] = cc[:, CON_NH:CON_NH + 1]

        def emit_boot(bb):
            bb.dma("S", G["c_par"][:, :], D["pars"][:, :])
            bb.dma("S", G["c_con"][:, :], D["cons"][:, :])
            bb.dma("S", G["c_idb"][:, :], D["consb"][:, :])
            return bb.dma("S", G["c_ma"][:, :], D["maska"][:, :])

        G["boot"] = bb.run_phase(emit_boot)

        def ln_stats(src, waits=()):
            pb = G["pb"]
            sq, mu, msq, var, rstd = (G[k] for k in
                                      ("sq", "mu", "msq", "var", "rstd"))
            tsq = [None, None]
            for dc in range(DC):
                xs = src[:, Q(dc)]
                w0 = list(waits) if dc == 0 else []
                tA = bb.a("activation", sq[:, Q(dc % 2)], xs, AF.Square,
                          waits=w0 + [tsq[dc % 2]], sig=True)
                bb.mm(pb[0][:, :], G["O1024"], xs, start=(dc == 0),
                      stop=(dc == DC - 1), waits=w0)
                tsq[dc % 2] = bb.mm(pb[1][:, :], G["O1024"], sq[:, Q(dc % 2)],
                                    start=(dc == 0), stop=(dc == DC - 1),
                                    waits=[tA], sig=True)
            tst = bb.tok("T")
            bb.v("tensor_copy", mu[:, :], pb[0][:, :], waits=[tst])
            bb.v("tensor_tensor", msq[:, :], mu[:, :], mu[:, :], ALU.mult)
            tv = bb.v("tensor_tensor", var[:, :], pb[1][:, :], msq[:, :],
                      ALU.subtract, sig=True)
            bb.a("activation", rstd[:, :], var[:, :], AF.Ln,
                 bias=G["EPSA"], waits=[tv])
            return bb.a("activation", rstd[:, :], rstd[:, :], AF.Exp,
                        scale=G["NHA"], sig=True)

        G["ln_stats"] = ln_stats
        lay_src = [D["x0"], D["sc_x"]]

        done = False
        with contextlib.ExitStack() as SLY:
            sly = lambda n, s, d=F32: SLY.enter_context(
                nc.sbuf_tensor(n, s, d))
            G["s_slab"] = sly("s_slab", [128, H * 128])
            G["s_hi"] = sly("s_hi", [128, 128], BF16)
            G["s_lo"] = sly("s_lo", [128, 128], BF16)
            G["srecv"] = sly("srecv", [128, H * 128])
            G["bias_wo"] = sly("bias_wo", [128, DC])
            for L in range(2):
                attn_layer(nc, bb, D, G, L, lay_src, stop)
                if stop in (f"L{L}", f"R1L{L}", f"R2L{L}"):
                    done = True
                    break
        if not done:
            tail(nc, bb, D, G)

        def emit_fin(bb):
            bb.wait("sync", bb.tok("dS"))
            bb.wait("gpsimd", bb.tok("dG"))
            bb.wait("scalar", bb.tok("dA"))

        bb.run_phase(emit_fin)
    return nc


def attn_layer(nc, bb, D, G, L, lay_src, stop=None):
    pb, pt = G["pb"], G["pt"]
    P1, MASK, O128, GAM, IDB = (G[k] for k in
                                ("P1", "MASK", "O128", "GAM", "IDB"))
    s_slab, s_hi, s_lo, srecv = (G[k] for k in
                                 ("s_slab", "s_hi", "s_lo", "srecv"))
    mu, msq, lnf, rstd = (G[k] for k in ("mu", "msq", "lnf", "rstd"))
    ln_stats = G["ln_stats"]
    boot = G["boot"]

    with contextlib.ExitStack() as S1:
        sb = lambda n, s, d=F32: S1.enter_context(
            nc.sbuf_tensor(n + str(L), s, d))
        wkh = sb("wkh", [128, H * DC * 128], BF16)
        wkl = sb("wkl", [128, H * DC * 128], BF16)
        wvh = sb("wvh", [128, H * DC * 128], BF16)
        wvl = sb("wvl", [128, H * DC * 128], BF16)
        wland = sb("wland", [128, 2 * DC * 128])
        xt = sb("xt", [128, DC * QS])
        l1h = sb("l1h", [128, DC * QS], BF16)
        l1l = sb("l1l", [128, DC * QS], BF16)
        ttab = sb("ttab", [128, 2 * 2 * QS])
        kfh = sb("kfh", [128, 2 * QS], BF16)
        kfl = sb("kfl", [128, 2 * QS], BF16)
        vfh = sb("vfh", [128, 2 * QS], BF16)
        vfl = sb("vfl", [128, 2 * QS], BF16)
        kth = sb("kth", [128, 2 * QS], BF16)
        ktl = sb("ktl", [128, 2 * QS], BF16)
        vth = sb("vth", [128, 2 * QS], BF16)
        vtl = sb("vtl", [128, 2 * QS], BF16)
        scA = sb("scA", [128, 2 * QS])
        scB = sb("scB", [128, 2 * QS])
        f1l = sb("f1l", [128, DC * 128])
        f1h = sb("f1h", [128, DC * 128], BF16)
        f1lo = sb("f1lo", [128, DC * 128], BF16)
        f2l = sb("f2l", [128, FC * 64])
        f2h = sb("f2h", [128, FC * 64], BF16)
        f2lo = sb("f2lo", [128, FC * 64], BF16)

        def emit_r1(bb):
            ent = [bb.tok("T"), bb.tok("V"), bb.tok("A")]
            bb.wait("sync", *ent)
            bb.wait("scalar", *ent)
            bb.wait("scalar", bb.tok("dG"))
            bb.wait("gpsimd", *ent)

            wtok = [None, None]
            for i, (key, dh, dl) in enumerate([("wk", wkh, wkl),
                                               ("wv", wvh, wvl)]):
                for h in range(H):
                    p = (i * H + h) % 2
                    sl = slice(p * DC * 128, (p + 1) * DC * 128)
                    dsl = slice(h * DC * 128, (h + 1) * DC * 128)
                    tl = bb.dma("S", wland[:, sl], D[key][L, h, :, :],
                                waits=[wtok[p], boot])
                    wtok[p] = bb.split(dh[:, dsl], dl[:, dsl], wland[:, sl],
                                       waits=[tl])
            w_r1 = bb.tok("V")

            jobs = []
            jst = {"f1": [None], "f2": [None]}

            def mk_job(kind, oc, half=0):
                def go():
                    if kind == "f1":
                        land, hi, lo = f1l, f1h, f1lo
                        src = D["fw1"][L, oc, :, :]
                        dsth = D["sc_w1"][L, oc, 0, :, :]
                        dstl = D["sc_w1"][L, oc, 1, :, :]
                    else:
                        n = FC * 64
                        land, hi, lo = f2l, f2h, f2lo
                        src = D["fw2"][L, oc, :, half * n:(half + 1) * n]
                        dsth = D["sc_w2"][L, oc, 0, :, half * n:(half + 1) * n]
                        dstl = D["sc_w2"][L, oc, 1, :, half * n:(half + 1) * n]
                    tl = bb.dma("S", land[:, :], src, waits=[jst[kind][0]])
                    ts = bb.split(hi[:, :], lo[:, :], land[:, :], waits=[tl])
                    bb.dma("G", dsth, hi[:, :], waits=[ts])
                    jst[kind][0] = bb.dma("G", dstl, lo[:, :])
                return go

            for oc in range(FC):
                jobs.append(mk_job("f1", oc))
            for oc in range(DC):
                jobs.append(mk_job("f2", oc, 0))
                jobs.append(mk_job("f2", oc, 1))

            bb.v("memset", s_slab[:, :], 0.0)

            itok = {}
            gh = 0
            prev_tks = prev_tvs = prev_tve = prev_tSm = None
            for qt in range(QT):
                txw = itok.get(gh - 1, []) + [boot] + (ent if gh == 0 else [])
                tx = bb.dma("A", xt[:, :], lay_src[L][qt, :, :], waits=txw)
                trs = ln_stats(xt, waits=[tx])
                for dc in range(DC):
                    bb.v("tensor_tensor", scA[:, Q(0)], xt[:, Q(dc)], mu[:, :],
                         ALU.subtract, waits=[trs] if dc == 0 else [])
                    bb.v("tensor_tensor", scA[:, Q(0)], scA[:, Q(0)], rstd[:, :],
                         ALU.mult)
                    bb.v("tensor_scalar", lnf[:, :], scA[:, Q(0)],
                         P1("ln1w", L * DC + dc), P1("ln1b", L * DC + dc),
                         ALU.mult, ALU.add)
                    bb.split(l1h[:, Q(dc)], l1l[:, Q(dc)], lnf[:, :])
                tl1 = bb.tok("V")
                bb.dma("G", D["sc_ln1"][qt, 0, :, :], l1h[:, :], waits=[tl1])
                bb.dma("G", D["sc_ln1"][qt, 1, :, :], l1l[:, :])

                for h in range(H):
                    p = h % 2
                    pw = itok.get(gh - 2, [])
                    bb.wait("tensor", *pw)
                    bb.wait("vector", *pw)
                    if jobs:
                        jobs.pop(0)()
                    psl = slice(p * 2 * QS, (p + 1) * 2 * QS)
                    ttb = bb.dma("A", ttab[:, psl],
                                 D["xtab"][h, qt, :, 2 * QS:4 * QS], waits=pw)
                    TAB = lambda i: ttab[:, p * 2 * QS + (i - 2) * QS:
                                         p * 2 * QS + (i - 1) * QS]
                    PP = lambda buf: buf[:, Q(p)]
                    KS = lambda buf, cci: buf[:, p * QS + cci * 128:
                                              p * QS + (cci + 1) * 128]

                    tK = None
                    for dc in range(DC):
                        hh = slice(h * DC * 128 + dc * 128,
                                   h * DC * 128 + (dc + 1) * 128)
                        tK = mm3(bb, pb[2][:, :], (wkh[:, hh], wkl[:, hh]),
                                 (l1h[:, Q(dc)], l1l[:, Q(dc)]),
                                 start=(dc == 0), stop=(dc == DC - 1),
                                 waits=[w_r1, tl1, prev_tks] if dc == 0 else [],
                                 sig=(dc == DC - 1))
                    bb.v("tensor_tensor", PP(scA), pb[2][:, :], TAB(2), ALU.mult,
                         waits=[tK, ttb])
                    bb.v("tensor_copy", PP(scB), pb[2][:, :])
                    bb.v("stream_shuffle", lnf[:, :], PP(scB), SHUF)
                    bb.v("tensor_tensor", lnf[:, :], lnf[:, :], TAB(3), ALU.mult)
                    bb.v("tensor_tensor", PP(scA), PP(scA), lnf[:, :], ALU.add)
                    tks = bb.split(PP(kfh), PP(kfl), PP(scA))
                    tV = None
                    for dc in range(DC):
                        hh = slice(h * DC * 128 + dc * 128,
                                   h * DC * 128 + (dc + 1) * 128)
                        tV = mm3(bb, pb[3][:, :], (wvh[:, hh], wvl[:, hh]),
                                 (l1h[:, Q(dc)], l1l[:, Q(dc)]),
                                 start=(dc == 0), stop=(dc == DC - 1),
                                 waits=[prev_tvs] if dc == 0 else [],
                                 sig=(dc == DC - 1))
                    bb.v("tensor_copy", PP(scB), pb[3][:, :], waits=[tV])
                    tvs = bb.split(PP(vfh), PP(vfl), PP(scB))
                    for cci in range(4):
                        bb.tr(pt[:, C1(cci)], KS(kfh, cci), IDB,
                              waits=[tks, prev_tve] if cci == 0 else [])
                        bb.tr(pt[:, C1(4 + cci)], KS(kfl, cci), IDB,
                              sig=(cci == 3))
                    tt1 = bb.tok("T")
                    bb.v("tensor_copy", PP(kth), pt[:, 0:QS], waits=[tt1])
                    tke = bb.v("tensor_copy", PP(ktl), pt[:, QS:2 * QS], sig=True)
                    for cci in range(4):
                        bb.tr(pt[:, C1(cci)], KS(vfh, cci), IDB,
                              waits=[tvs, tke] if cci == 0 else [])
                        bb.tr(pt[:, C1(4 + cci)], KS(vfl, cci), IDB,
                              sig=(cci == 3))
                    tt2 = bb.tok("T")
                    bb.v("tensor_copy", PP(vth), pt[:, 0:QS], waits=[tt2])
                    tve = bb.v("tensor_copy", PP(vtl), pt[:, QS:2 * QS], sig=True)
                    for cci in range(4):
                        bb.mm(pb[5][:, 0:128], KS(kth, cci), KS(vth, cci),
                              start=True, stop=False,
                              waits=[tve, prev_tSm] if cci == 0 else [prev_tSm])
                        bb.mm(pb[5][:, 0:128], KS(ktl, cci), KS(vth, cci),
                              start=False, stop=False)
                        tS = bb.mm(pb[5][:, 0:128], KS(kth, cci), KS(vtl, cci),
                                   start=False, stop=True, sig=True)
                        bb.v("tensor_tensor", s_slab[:, C1(h)], s_slab[:, C1(h)],
                             pb[5][:, 0:128], ALU.add, waits=[tS])
                        prev_tSm = bb.v("tensor_tensor", s_slab[:, C1(h)],
                                        s_slab[:, C1(h)], GAM(h), ALU.mult,
                                        sig=True)
                    t_all = bb.tok("V")
                    bb.dma("G", D["sc_kv"][h, qt, 0, :, :], PP(kfh),
                           waits=[t_all])
                    bb.dma("G", D["sc_kv"][h, qt, 1, :, :], PP(kfl))
                    bb.dma("G", D["sc_kv"][h, qt, 2, :, :], PP(kth))
                    bb.dma("G", D["sc_kv"][h, qt, 3, :, :], PP(ktl))
                    bb.dma("G", D["sc_kv"][h, qt, 4, :, :], PP(vth))
                    bb.dma("G", D["sc_kv"][h, qt, 5, :, :], PP(vtl))
                    prev_tks, prev_tvs, prev_tve = tks, tvs, tve
                    itok[gh] = bb.snapc()
                    gh += 1
            while jobs:
                jobs.pop(0)()

            tSfin = bb.tok("V")
            bb.dma("G", D["cc_in"][:, :], s_slab[:, :], waits=[tSfin])
            bb.wait("gpsimd", bb.tok("dG"))
            tcc = bb.collective(
                "AllGather", ALU.bypass,
                [[0, 1], [2, 3], [4, 5], [6, 7]],
                [D["cc_in"].ap().opt()], [D["cc_out"].ap().opt()])
            trecv = bb.dma("G", srecv[:, :], D["cc_out"][0:128, :], waits=[tcc])
            return bb.v("tensor_scalar", s_slab[:, :], srecv[:, :],
                        G["c_ma"][:, 0:1], None, ALU.mult, waits=[trecv],
                        sig=True)

        tinit = bb.run_phase(emit_r1)

    if stop == f"R1L{L}":
        return
    with contextlib.ExitStack() as S2:
        sb = lambda n, s, d=F32: S2.enter_context(
            nc.sbuf_tensor(n + "r2" + str(L), s, d))
        wqh = sb("wqh", [128, H * DC * 128], BF16)
        wql = sb("wql", [128, H * DC * 128], BF16)
        wgh = sb("wgh", [128, 2 * DC * 128], BF16)
        wgl = sb("wgl", [128, 2 * DC * 128], BF16)
        wland = sb("wland", [128, 2 * DC * 128])
        l1h = sb("l1h", [128, DC * QS], BF16)
        l1l = sb("l1l", [128, DC * QS], BF16)
        ttab = sb("ttab", [128, 2 * 2 * QS])
        kfh = sb("kfh", [128, 2 * QS], BF16)
        kfl = sb("kfl", [128, 2 * QS], BF16)
        kth = sb("kth", [128, 2 * QS], BF16)
        ktl = sb("ktl", [128, 2 * QS], BF16)
        vth = sb("vth", [128, 2 * QS], BF16)
        vtl = sb("vtl", [128, 2 * QS], BF16)
        qph = sb("qph", [128, 2 * QS], BF16)
        qpl = sb("qpl", [128, 2 * QS], BF16)
        sch = sb("sch", [128, 4 * 128], BF16)
        scl = sb("scl", [128, 4 * 128], BF16)
        scA = sb("scA", [128, 2 * QS])
        scB = sb("scB", [128, 2 * QS])
        ymu = sb("ymu", [128, H * QS])
        vst = sb("vst", [128, H * QS])
        gyh = sb("gyh", [128, H * QS], BF16)
        gyl = sb("gyl", [128, H * QS], BF16)
        wos = sb("wos", [128, DC * 128])
        wohi = sb("wohi", [128, DC * 128], BF16)
        wolo = sb("wolo", [128, DC * 128], BF16)
        woev = sb("woev", [128, QS])
        xt2 = sb("xt2", [128, 2 * QS])
        woland = wland

        def emit_r2(bb):
            ent = [bb.tok("T"), bb.tok("V"), bb.tok("A")]
            bb.wait("sync", *ent)
            bb.wait("scalar", *ent)
            bb.wait("scalar", bb.tok("dG"))
            bb.wait("gpsimd", *ent)

            wl_tok = [None, None]
            for h in range(H):
                p = h % 2
                sl = slice(p * DC * 128, (p + 1) * DC * 128)
                dsl = slice(h * DC * 128, (h + 1) * DC * 128)
                tl = bb.dma("S", wland[:, sl], D["wq"][L, h, :, :],
                            waits=[wl_tok[p]])
                wl_tok[p] = bb.split(wqh[:, dsl], wql[:, dsl], wland[:, sl],
                                     waits=[tl])
            w_r2 = bb.tok("V")

            itok = {}
            gh = 0
            last_y = None
            prev_tmu = None
            prev_tvst = None
            prev_wo_ev = [None, None]
            wg_cons = [None, None]
            wl_bias = [None, None]
            for qt in range(QT):
                pv = itok.get(gh - 1, []) + (ent if gh == 0 else [])
                tlh = bb.dma("A", l1h[:, :], D["sc_ln1"][qt, 0, :, :], waits=pv)
                tll = bb.dma("A", l1l[:, :], D["sc_ln1"][qt, 1, :, :])
                for h in range(H):
                    p = h % 2
                    pw = itok.get(gh - 2, [])
                    bb.wait("tensor", *pw)
                    bb.wait("vector", *pw)
                    psl = slice(p * 2 * QS, (p + 1) * 2 * QS)
                    ttb = bb.dma("A", ttab[:, psl],
                                 D["xtab"][h, qt, :, 0:2 * QS], waits=pw)
                    bb.dma("A", kfh[:, Q(p)], D["sc_kv"][h, qt, 0, :, :])
                    bb.dma("A", kfl[:, Q(p)], D["sc_kv"][h, qt, 1, :, :])
                    bb.dma("A", kth[:, Q(p)], D["sc_kv"][h, qt, 2, :, :])
                    bb.dma("A", ktl[:, Q(p)], D["sc_kv"][h, qt, 3, :, :])
                    bb.dma("A", vth[:, Q(p)], D["sc_kv"][h, qt, 4, :, :])
                    tk5 = bb.dma("A", vtl[:, Q(p)], D["sc_kv"][h, qt, 5, :, :])
                    TAB = lambda i: ttab[:, p * 2 * QS + i * QS:
                                         p * 2 * QS + (i + 1) * QS]
                    PP = lambda buf: buf[:, Q(p)]
                    KS = lambda buf, cci: buf[:, p * QS + cci * 128:
                                              p * QS + (cci + 1) * 128]

                    tQ = None
                    for dc in range(DC):
                        hh = slice(h * DC * 128 + dc * 128,
                                   h * DC * 128 + (dc + 1) * 128)
                        tQ = mm3(bb, pb[2][:, :], (wqh[:, hh], wql[:, hh]),
                                 (l1h[:, Q(dc)], l1l[:, Q(dc)]),
                                 start=(dc == 0), stop=(dc == DC - 1),
                                 waits=[w_r2, tlh, tll, prev_tmu]
                                 if dc == 0 else [], sig=(dc == DC - 1))
                    bb.v("tensor_tensor", PP(scA), pb[2][:, :], TAB(0), ALU.mult,
                         waits=[tQ, ttb])
                    bb.v("tensor_copy", PP(scB), pb[2][:, :])
                    bb.v("stream_shuffle", lnf[:, :], PP(scB), SHUF)
                    bb.v("tensor_tensor", lnf[:, :], lnf[:, :], TAB(1), ALU.mult)
                    bb.v("tensor_tensor", PP(scA), PP(scA), lnf[:, :], ALU.add)
                    tqs = bb.split(PP(qph), PP(qpl), PP(scA))

                    for cci in range(4):
                        if cci == 0:
                            bb.split(s_hi[:, :], s_lo[:, :], s_slab[:, C1(h)],
                                     waits=[last_y, tinit])
                        bb.mm(pb[4][:, C1(cci)], KS(kfh, cci), KS(qph, cci),
                              start=True, stop=False,
                              waits=[tk5, tqs, bb.tok("V")] if cci == 0 else [])
                        bb.mm(pb[4][:, C1(cci)], KS(kfh, cci), KS(qpl, cci),
                              start=False, stop=False)
                        tsc = bb.mm(pb[4][:, C1(cci)], KS(kfl, cci),
                                    KS(qph, cci), start=False, stop=True,
                                    sig=True)
                        bb.v("tensor_tensor", scB[:, p * QS:p * QS + 128],
                             pb[4][:, C1(cci)], MASK, ALU.mult, waits=[tsc])
                        tspl = bb.split(sch[:, C1(cci)], scl[:, C1(cci)],
                                        scB[:, p * QS:p * QS + 128])
                        bb.mm(pb[5][:, C1(cci)], KS(vth, cci), sch[:, C1(cci)],
                              start=True, stop=False, waits=[tspl])
                        bb.mm(pb[5][:, C1(cci)], KS(vth, cci), scl[:, C1(cci)],
                              start=False, stop=False)
                        bb.mm(pb[5][:, C1(cci)], KS(vtl, cci), sch[:, C1(cci)],
                              start=False, stop=False)
                        bb.mm(pb[5][:, C1(cci)], s_hi[:, :], KS(qph, cci),
                              start=False, stop=False)
                        bb.mm(pb[5][:, C1(cci)], s_hi[:, :], KS(qpl, cci),
                              start=False, stop=False)
                        last_y = bb.mm(pb[5][:, C1(cci)], s_lo[:, :],
                                       KS(qph, cci), start=False, stop=True,
                                       sig=True)
                        bb.mm(pb[6][:, 0:128], KS(kth, cci), KS(vth, cci),
                              start=True, stop=False)
                        bb.mm(pb[6][:, 0:128], KS(ktl, cci), KS(vth, cci),
                              start=False, stop=False)
                        tS = bb.mm(pb[6][:, 0:128], KS(kth, cci), KS(vtl, cci),
                                   start=False, stop=True, sig=True)
                        bb.v("tensor_tensor", s_slab[:, C1(h)], s_slab[:, C1(h)],
                             pb[6][:, 0:128], ALU.add, waits=[tS])
                        bb.v("tensor_tensor", s_slab[:, C1(h)], s_slab[:, C1(h)],
                             GAM(h), ALU.mult)
                        if cci < 3:
                            bb.split(s_hi[:, :], s_lo[:, :], s_slab[:, C1(h)],
                                     waits=[last_y])
                    tyf = bb.v("tensor_copy", PP(scA), pb[5][:, :],
                               waits=[last_y], sig=True)
                    bb.mm(pb[2][:, :], O128, PP(scA), start=True, stop=True,
                          waits=[tyf])
                    ty2 = bb.a("activation", PP(scB), PP(scA), AF.Square,
                               waits=[tyf], sig=True)
                    tm2 = bb.mm(pb[3][:, :], O128, PP(scB), start=True,
                                stop=True, waits=[ty2, prev_tvst], sig=True)
                    prev_tmu = bb.v("tensor_copy", mu[:, :], pb[2][:, :],
                                    waits=[tm2], sig=True)
                    bb.v("tensor_tensor", ymu[:, h * QS:(h + 1) * QS], PP(scA),
                         mu[:, :], ALU.subtract)
                    bb.v("tensor_tensor", msq[:, :], mu[:, :], mu[:, :],
                         ALU.mult)
                    prev_tvst = bb.v("tensor_tensor",
                                     vst[:, h * QS:(h + 1) * QS],
                                     pb[3][:, :], msq[:, :], ALU.subtract,
                                     sig=True)
                    itok[gh] = bb.snapc()
                    gh += 1
                tv1 = bb.tok("V")
                bb.a("activation", vst[:, :], vst[:, :], AF.Ln,
                     bias=G["EPSA"], waits=[tv1])
                trstd = bb.a("activation", vst[:, :], vst[:, :], AF.Exp,
                             scale=G["NHA"], sig=True)
                tgp = None
                et3 = {}
                for h in range(H):
                    p = h % 2
                    wsl = slice(p * DC * 128, (p + 1) * DC * 128)
                    tl = bb.dma("S", wland[:, wsl], D["wg"][L, h, :, :],
                                waits=[wl_tok[p], wl_bias[p]])
                    wl_tok[p] = tgs = bb.split(wgh[:, wsl], wgl[:, wsl],
                                               wland[:, wsl],
                                               waits=[tl, wg_cons[p]])
                    tG = None
                    for dc in range(DC):
                        gsl = slice(p * DC * 128 + dc * 128,
                                    p * DC * 128 + (dc + 1) * 128)
                        tG = mm3(bb, pb[2][:, :], (wgh[:, gsl], wgl[:, gsl]),
                                 (l1h[:, Q(dc)], l1l[:, Q(dc)]),
                                 start=(dc == 0), stop=(dc == DC - 1),
                                 waits=[tgp if tgp else trstd, tgs]
                                 if dc == 0 else [], sig=(dc == DC - 1))
                    wg_cons[p] = tG
                    tsil = bb.a("activation", scA[:, Q(p)], pb[2][:, :], AF.Silu,
                                waits=[tG, et3.get(h - 2)], sig=True)
                    tgp = tsil
                    hs = slice(h * QS, (h + 1) * QS)
                    bb.v("tensor_tensor", scB[:, Q(0)], ymu[:, hs], vst[:, hs],
                         ALU.mult, waits=[trstd, tsil])
                    bb.v("tensor_tensor", scB[:, Q(0)], scB[:, Q(0)],
                         scA[:, Q(p)], ALU.mult)
                    et3[h] = bb.split(gyh[:, hs], gyl[:, hs], scB[:, Q(0)])
                tgy = bb.tok("V")
                prev_two = None
                for oc in range(DC):
                    p = oc % 2
                    wsl = slice(p * DC * 128, (p + 1) * DC * 128)
                    tl = bb.dma("S", wland[:, wsl], D["wo"][L, oc, :, :],
                                waits=[wl_tok[p], wl_bias[p]])
                    if qt == 0:
                        tb = None
                        for vc in range(DC):
                            tb = bb.mm(pb[3][:, oc:oc + 1],
                                       wland[:, p * DC * 128 + vc * 128:
                                             p * DC * 128 + (vc + 1) * 128],
                                       P1("gnb", L * DC + vc),
                                       start=(vc == 0), stop=(vc == DC - 1),
                                       waits=[tl, tgy] if vc == 0 else [],
                                       sig=(vc == DC - 1))
                        wl_bias[p] = tb
                        bb.v("tensor_copy", G["bias_wo"][:, oc:oc + 1],
                             pb[3][:, oc:oc + 1], waits=[tb])
                    for vc in range(DC):
                        bb.v("tensor_scalar", wos[:, C1(vc)],
                             wland[:, p * DC * 128 + vc * 128:
                                   p * DC * 128 + (vc + 1) * 128],
                             P1("gnw", L * DC + vc), None, ALU.mult,
                             waits=[tl, prev_two] if vc == 0 else [])
                    tws = bb.split(wohi[:, :], wolo[:, :], wos[:, :])
                    wl_tok[p] = tws
                    pbank = pb[0] if oc % 2 == 0 else pb[1]
                    txr = bb.dma("A", xt2[:, Q(p)], lay_src[L][qt, :, Q(oc)],
                                 waits=[prev_wo_ev[p]])
                    two = None
                    for vc in range(DC):
                        two = mm3(bb, pbank[:, :],
                                  (wohi[:, C1(vc)], wolo[:, C1(vc)]),
                                  (gyh[:, vc * QS:(vc + 1) * QS],
                                   gyl[:, vc * QS:(vc + 1) * QS]),
                                  start=(vc == 0), stop=(vc == DC - 1),
                                  waits=[tws, tgy, prev_wo_ev[p]]
                                  if vc == 0 else [], sig=(vc == DC - 1))
                    prev_two = two
                    bb.v("tensor_scalar", woev[:, :], pbank[:, :],
                         G["bias_wo"][:, oc:oc + 1], None, ALU.add,
                         waits=[two, txr])
                    tev = bb.v("tensor_tensor", xt2[:, Q(p)], woev[:, :],
                               xt2[:, Q(p)], ALU.add, sig=True)
                    prev_wo_ev[p] = bb.dma("G", D["sc_y"][qt, :, Q(oc)],
                                           xt2[:, Q(p)], waits=[tev])
                itok[gh - 1] = bb.snapc()

        bb.run_phase(emit_r2)

    if stop == f"R2L{L}":
        return
    with contextlib.ExitStack() as S3:
        sb = lambda n, s, d=F32: S3.enter_context(
            nc.sbuf_tensor(n + "ffn" + str(L), s, d))
        yt = sb("yt", [128, DC * QS])
        l2h = sb("l2h", [128, DC * QS], BF16)
        l2l = sb("l2l", [128, DC * QS], BF16)
        w1h = sb("w1h", [128, 2 * DC * 128], BF16)
        w1l = sb("w1l", [128, 2 * DC * 128], BF16)
        w2h = sb("w2h", [128, FC * 128], BF16)
        w2l = sb("w2l", [128, FC * 128], BF16)
        hh_ = sb("hh", [128, FC * QS], BF16)
        hl_ = sb("hl", [128, FC * QS], BF16)
        gsc = sb("gsc", [128, 2 * QS])
        esc = sb("esc", [128, QS])

        def emit_ffn(bb):
            ent = [bb.tok("T"), bb.tok("V"), bb.tok("A")]
            bb.wait("sync", *ent)
            bb.wait("sync", bb.tok("dG"))
            bb.wait("scalar", *ent)
            bb.wait("scalar", bb.tok("dG"))
            bb.wait("gpsimd", *ent)

            ln_stats = G["ln_stats"]
            prev = []
            for qt in range(QT):
                ty = bb.dma("A", yt[:, :], D["sc_y"][qt, :, :],
                            waits=prev + (ent if qt == 0 else []))
                trs = ln_stats(yt, waits=[ty])
                for dc in range(DC):
                    bb.v("tensor_tensor", gsc[:, Q(0)], yt[:, Q(dc)], mu[:, :],
                         ALU.subtract, waits=[trs] if dc == 0 else [])
                    bb.v("tensor_tensor", gsc[:, Q(0)], gsc[:, Q(0)], rstd[:, :],
                         ALU.mult)
                    bb.v("tensor_scalar", lnf[:, :], gsc[:, Q(0)],
                         P1("ln2w", L * DC + dc), P1("ln2b", L * DC + dc),
                         ALU.mult, ALU.add)
                    bb.split(l2h[:, Q(dc)], l2l[:, Q(dc)], lnf[:, :])
                tl2 = bb.tok("V")
                wtok = [None, None]
                etok = {}
                for oc in range(FC):
                    p = oc % 2
                    wsl = slice(p * DC * 128, (p + 1) * DC * 128)
                    bb.dma("S", w1h[:, wsl], D["sc_w1"][L, oc, 0, :, :],
                           waits=[wtok[p]])
                    twl = bb.dma("S", w1l[:, wsl], D["sc_w1"][L, oc, 1, :, :])
                    pbank = pb[2 + (oc % 4)]
                    tmm = None
                    for dc in range(DC):
                        tmm = mm3(bb, pbank[:, :],
                                  (w1h[:, p * DC * 128 + dc * 128:
                                       p * DC * 128 + (dc + 1) * 128],
                                   w1l[:, p * DC * 128 + dc * 128:
                                       p * DC * 128 + (dc + 1) * 128]),
                                  (l2h[:, Q(dc)], l2l[:, Q(dc)]),
                                  start=(dc == 0), stop=(dc == DC - 1),
                                  waits=[twl, tl2, etok.get(oc - 4)]
                                  if dc == 0 else [], sig=(dc == DC - 1))
                    wtok[p] = tmm
                    tg = bb.a("activation", gsc[:, Q(oc % 2)], pbank[:, :],
                              AF.Gelu, bias=P1("fb1", L * FC + oc),
                              waits=[tmm, etok.get(oc - 2)], sig=True)
                    etok[oc] = bb.split(hh_[:, Q(oc)], hl_[:, Q(oc)],
                                        gsc[:, Q(oc % 2)], waits=[tg])
                th_all = bb.tok("V")
                w2tok = None
                etok2 = {}
                for oc in range(DC):
                    bb.dma("S", w2h[:, :], D["sc_w2"][L, oc, 0, :, :],
                           waits=[w2tok])
                    twl = bb.dma("S", w2l[:, :], D["sc_w2"][L, oc, 1, :, :])
                    pbank = pb[2 + (oc % 4)]
                    tmm = None
                    for fc in range(FC):
                        tmm = mm3(bb, pbank[:, :],
                                  (w2h[:, C1(fc)], w2l[:, C1(fc)]),
                                  (hh_[:, Q(fc)], hl_[:, Q(fc)]),
                                  start=(fc == 0), stop=(fc == FC - 1),
                                  waits=[twl, th_all, etok2.get(oc - 4)]
                                  if fc == 0 else [], sig=(fc == FC - 1))
                    w2tok = tmm
                    bb.v("tensor_scalar", esc[:, :], pbank[:, :],
                         P1("fb2", L * DC + oc), None, ALU.add, waits=[tmm])
                    etok2[oc] = bb.v("tensor_tensor", yt[:, Q(oc)], esc[:, :],
                                     yt[:, Q(oc)], ALU.add, sig=True)
                    bb.dma("G", D["sc_x"][qt, :, Q(oc)], yt[:, Q(oc)],
                           waits=[etok2[oc]])
                prev = bb.snapc()

        bb.run_phase(emit_ffn)


def tail(nc, bb, D, G):
    pb = G["pb"]
    mu, rstd = G["mu"], G["rstd"]
    ln_stats = G["ln_stats"]
    P1 = G["P1"]

    with contextlib.ExitStack() as S4:
        sb = lambda n, s, d=F32: S4.enter_context(nc.sbuf_tensor(n, s, d))
        xt = sb("xtt", [128, DC * QS])
        xta = sb("xta", [128, DC * QS])
        osc = sb("osc", [128, QS])
        def emit_topln(bb):
            ent = [bb.tok("T"), bb.tok("V"), bb.tok("A")]
            bb.wait("scalar", *ent)
            bb.wait("scalar", bb.tok("dG"))
            bb.wait("gpsimd", *ent)
            prev = []
            for qt in range(QT):
                t1 = bb.dma("A", xt[:, :], D["sc_x"][qt, :, :],
                            waits=prev + (ent if qt == 0 else []))
                t2 = bb.dma("A", xta[:, :], D["x0"][qt, :, :])
                tadd = None
                for dc in range(DC):
                    tadd = bb.v("tensor_tensor", xt[:, Q(dc)], xt[:, Q(dc)],
                                xta[:, Q(dc)], ALU.add,
                                waits=[t1, t2] if dc == 0 else [], sig=True)
                trs = ln_stats(xt, waits=[tadd])
                td = None
                for dc in range(DC):
                    bb.v("tensor_tensor", osc[:, :], xt[:, Q(dc)], mu[:, :],
                         ALU.subtract,
                         waits=[trs, td] if dc == 0 else [td])
                    tn = bb.v("tensor_tensor", osc[:, :], osc[:, :], rstd[:, :],
                              ALU.mult, sig=True)
                    td = bb.dma("G", D["sc_x1"][qt, :, Q(dc)], osc[:, :],
                                waits=[tn])
                prev = bb.snapc()

        bb.run_phase(emit_topln)

    with contextlib.ExitStack() as S5:
        sb = lambda n, s, d=F32: S5.enter_context(nc.sbuf_tensor(n, s, d))
        x1t = sb("x1t", [128, DC * QS])
        thh = sb("thh", [128, FC * QS], BF16)
        thl = sb("thl", [128, FC * QS], BF16)
        t1h = sb("t1h", [128, FC * QS], BF16)
        t1l = sb("t1l", [128, FC * QS], BF16)
        x1h = t1h
        x1l = t1l
        wa = sb("wa", [128, 2 * DC * 128], BF16)
        wb = sb("wb", [128, 2 * DC * 128], BF16)
        wbig = sb("wbig", [128, 2 * FC * 128], BF16)
        sA = sb("sA", [128, QS])
        sB = sb("sB", [128, QS])
        def emit_tern(bb):
            ent = [bb.tok("T"), bb.tok("V"), bb.tok("A")]
            bb.wait("scalar", *ent)
            bb.wait("scalar", bb.tok("dG"))
            bb.wait("gpsimd", *ent)
            bb.wait("sync", *ent)
            prev = []
            wtok_ab = [None, None]
            wtok_big = [None, None]
            for qt in range(QT):
                tx = bb.dma("A", x1t[:, :], D["sc_x1"][qt, :, :],
                            waits=prev + (ent if qt == 0 else []))
                for dc in range(DC):
                    bb.split(x1h[:, Q(dc)], x1l[:, Q(dc)], x1t[:, Q(dc)],
                             waits=[tx] if dc == 0 else [])
                tx1s = bb.tok("V")
                etok = {}
                for oc in range(FC):
                    p = oc % 2
                    wsl = slice(p * DC * 128, (p + 1) * DC * 128)
                    bb.dma("G", wa[:, wsl], D["tw1"][oc, :, :],
                           waits=[wtok_ab[p]])
                    twl = bb.dma("G", wb[:, wsl], D["tw2"][oc, :, :])
                    pa = pb[2 + (oc % 2) * 2]
                    pk = pb[3 + (oc % 2) * 2]
                    for dc in range(DC):
                        ws1 = slice(p * DC * 128 + dc * 128,
                                    p * DC * 128 + (dc + 1) * 128)
                        bb.mm(pa[:, :], wa[:, ws1], x1h[:, Q(dc)],
                              start=(dc == 0), stop=False,
                              waits=[twl, tx1s, etok.get(oc - 2)]
                              if dc == 0 else [])
                        bb.mm(pa[:, :], wa[:, ws1], x1l[:, Q(dc)],
                              start=False, stop=(dc == DC - 1))
                    tmm = None
                    for dc in range(DC):
                        ws1 = slice(p * DC * 128 + dc * 128,
                                    p * DC * 128 + (dc + 1) * 128)
                        bb.mm(pk[:, :], wb[:, ws1], x1h[:, Q(dc)],
                              start=(dc == 0), stop=False)
                        tmm = bb.mm(pk[:, :], wb[:, ws1], x1l[:, Q(dc)],
                                    start=False, stop=(dc == DC - 1),
                                    sig=(dc == DC - 1))
                    wtok_ab[p] = tmm
                    ta = bb.a("activation", sA[:, :], pa[:, :], AF.Silu,
                              scale=P1("w1s", oc),
                              waits=[tmm, etok.get(oc - 1)], sig=True)
                    bb.v("tensor_scalar", sB[:, :], pk[:, :], P1("w2s", oc),
                         None, ALU.mult, waits=[tmm])
                    bb.v("tensor_tensor", sB[:, :], sA[:, :], sB[:, :], ALU.mult,
                         waits=[ta])
                    etok[oc] = bb.split(thh[:, Q(oc)], thl[:, Q(oc)], sB[:, :])
                tT1 = bb.tok("V")
                etok = {}
                for oc in range(FC):
                    p = oc % 2
                    wsl = slice(p * FC * 128, (p + 1) * FC * 128)
                    twl = bb.dma("G", wbig[:, wsl], D["th1"][oc, :, :],
                                 waits=[wtok_big[p]])
                    pbank = pb[2 + (oc % 4)]
                    tmm = None
                    for ic in range(FC):
                        ws1 = slice(p * FC * 128 + ic * 128,
                                    p * FC * 128 + (ic + 1) * 128)
                        bb.mm(pbank[:, :], wbig[:, ws1], thh[:, Q(ic)],
                              start=(ic == 0), stop=False,
                              waits=[twl, tT1, etok.get(oc - 4)]
                              if ic == 0 else [])
                        tmm = bb.mm(pbank[:, :], wbig[:, ws1], thl[:, Q(ic)],
                                    start=False, stop=(ic == FC - 1),
                                    sig=(ic == FC - 1))
                    wtok_big[p] = tmm
                    tg = bb.a("activation", sA[:, :], pbank[:, :], AF.Gelu,
                              scale=P1("h1s", oc),
                              waits=[tmm, etok.get(oc - 1)], sig=True)
                    etok[oc] = bb.split(t1h[:, Q(oc)], t1l[:, Q(oc)], sA[:, :],
                                        waits=[tg])
                tT2 = bb.tok("V")
                etok = {}
                for oc in range(FC):
                    p = oc % 2
                    wsl = slice(p * FC * 128, (p + 1) * FC * 128)
                    twl = bb.dma("G", wbig[:, wsl], D["th2"][oc, :, :],
                                 waits=[wtok_big[p]])
                    pbank = pb[2 + (oc % 4)]
                    tmm = None
                    for ic in range(FC):
                        ws1 = slice(p * FC * 128 + ic * 128,
                                    p * FC * 128 + (ic + 1) * 128)
                        bb.mm(pbank[:, :], wbig[:, ws1], t1h[:, Q(ic)],
                              start=(ic == 0), stop=False,
                              waits=[twl, tT2, etok.get(oc - 4)]
                              if ic == 0 else [])
                        tmm = bb.mm(pbank[:, :], wbig[:, ws1], t1l[:, Q(ic)],
                                    start=False, stop=(ic == FC - 1),
                                    sig=(ic == FC - 1))
                    wtok_big[p] = tmm
                    tg = bb.a("activation", sA[:, :], pbank[:, :], AF.Gelu,
                              scale=P1("h2s", oc),
                              waits=[tmm, etok.get(oc - 1)], sig=True)
                    etok[oc] = bb.split(thh[:, Q(oc)], thl[:, Q(oc)], sA[:, :],
                                        waits=[tg])
                tT3 = bb.tok("V")
                tprev = None
                for oc in range(DC):
                    p = oc % 2
                    wsl = slice(p * FC * 128, (p + 1) * FC * 128)
                    twl = bb.dma("G", wbig[:, wsl], D["tw3"][oc, :, :],
                                 waits=[wtok_big[p]])
                    tmm = None
                    for ic in range(FC):
                        ws1 = slice(p * FC * 128 + ic * 128,
                                    p * FC * 128 + (ic + 1) * 128)
                        bb.mm(pb[6][:, :], wbig[:, ws1], thh[:, Q(ic)],
                              start=(ic == 0), stop=False,
                              waits=[twl, tT3, tprev] if ic == 0 else [])
                        tmm = bb.mm(pb[6][:, :], wbig[:, ws1], thl[:, Q(ic)],
                                    start=False, stop=(ic == FC - 1),
                                    sig=(ic == FC - 1))
                    wtok_big[p] = tmm
                    bb.v("tensor_scalar", sA[:, :], pb[6][:, :], P1("w3s", oc),
                         None, ALU.mult, waits=[tmm])
                    tprev = bb.v("tensor_tensor", x1t[:, Q(oc)], sA[:, :],
                                 x1t[:, Q(oc)], ALU.add, sig=True)
                trs = ln_stats(x1t, waits=[tprev])
                td = None
                for dc in range(DC):
                    bb.v("tensor_tensor", sA[:, :], x1t[:, Q(dc)], mu[:, :],
                         ALU.subtract,
                         waits=[trs, td] if dc == 0 else [td])
                    tn = bb.v("tensor_tensor", sA[:, :], sA[:, :], rstd[:, :],
                              ALU.mult, sig=True)
                    td = bb.dma("S", D["out"][qt, :, Q(dc)], sA[:, :],
                                waits=[tn])
                prev = bb.snapc()

        bb.run_phase(emit_tern)


_GAMMAS = 1.0 - np.exp(np.linspace(np.log(1 / 32), np.log(1 / 512), H))


def host_tables(parity):
    g = np.arange(parity * T, parity * T + T, dtype=np.float64)
    base = (np.arange(0, 128, 2, dtype=np.float64) + 0.4 * 128) / (1.4 * 128)
    sc = base[None, :] ** (g / 512.0)[:, None]
    inv = 1.0 / 10000.0 ** (np.arange(64, dtype=np.float64) / 64)
    ang = g[:, None] * inv[None, :]
    sin, cos = np.sin(ang), np.cos(ang)
    loc = (np.arange(T) % CH).astype(np.float64)
    dupi = np.arange(128) // 2
    sgn = np.where(np.arange(128) % 2 == 0, -1.0, 1.0)
    tabs = np.zeros((H, 4, 128, T), dtype=np.float64)
    for h in range(H):
        gq = _GAMMAS[h] ** loc
        gk = _GAMMAS[h] ** (-loc)
        tabs[h, 0] = (cos * sc)[:, dupi].T * gq[None, :]
        tabs[h, 1] = (sin * sc)[:, dupi].T * gq[None, :] * sgn[:, None]
        tabs[h, 2] = (cos / sc)[:, dupi].T * gk[None, :]
        tabs[h, 3] = (sin / sc)[:, dupi].T * gk[None, :] * sgn[:, None]
    t4 = tabs.reshape(H, 4, 128, QT, QS).transpose(0, 3, 2, 1, 4)
    return np.ascontiguousarray(
        t4.reshape(H, QT, 128, 4 * QS).astype(np.float32))


def host_consts():
    cons = np.zeros((128, NCON), dtype=np.float32)
    m = (np.arange(128)[None, :] >= np.arange(128)[:, None])
    cons[:, CON_MASK:CON_MASK + 128] = m.astype(np.float32)
    cons[:, CON_O128:CON_O128 + 128] = 1.0 / 128
    cons[:, CON_O1024:CON_O1024 + 128] = 1.0 / 1024
    for h in range(H):
        cons[:, CON_GAM + h * 128:CON_GAM + (h + 1) * 128] = np.float32(
            _GAMMAS[h] ** 128)
    cons[:, CON_EPS] = EPS
    cons[:, CON_NH] = -0.5
    return cons


def host_pars(I):
    p = np.zeros((128, NPAR), dtype=np.float32)

    def put(nm, arr):
        a = np.asarray(arr)
        p[:, _PAR[nm]:_PAR[nm] + a.shape[1]] = a

    put("ln1w", I["L0_attn_ln1_w"].reshape(2, 8, 128).transpose(2, 0, 1)
        .reshape(128, 16))
    put("ln1b", I["L0_attn_ln1_b"].reshape(2, 8, 128).transpose(2, 0, 1)
        .reshape(128, 16))
    put("ln2w", I["L0_attn_ln2_w"].reshape(2, 8, 128).transpose(2, 0, 1)
        .reshape(128, 16))
    put("ln2b", I["L0_attn_ln2_b"].reshape(2, 8, 128).transpose(2, 0, 1)
        .reshape(128, 16))
    put("gnw", I["L0_attn_gn_w"].reshape(2, 8, 128).transpose(2, 0, 1)
        .reshape(128, 16))
    put("gnb", I["L0_attn_gn_b"].reshape(2, 8, 128).transpose(2, 0, 1)
        .reshape(128, 16))
    put("fb1", I["L0_attn_ffn_b1"].reshape(2, 32, 128).transpose(2, 0, 1)
        .reshape(128, 64))
    put("fb2", I["L0_attn_ffn_b2"].reshape(2, 8, 128).transpose(2, 0, 1)
        .reshape(128, 16))
    put("w1s", I["L0_ffn_w1_s"].reshape(32, 128).T)
    put("w2s", I["L0_ffn_w2_s"].reshape(32, 128).T)
    put("h1s", I["L0_ffn_h1_s"].reshape(32, 128).T)
    put("h2s", I["L0_ffn_h2_s"].reshape(32, 128).T)
    put("w3s", I["L0_ffn_w3_s"].reshape(8, 128).T)
    return p


def host_weights(I):
    w = {}

    def c(a):
        return np.ascontiguousarray(a, dtype=np.float32)

    for nm, key in (("wq", "L0_attn_Wq"), ("wk", "L0_attn_Wk"),
                    ("wv", "L0_attn_Wv")):
        a = I[key].reshape(2, H, DC, 128, 128).transpose(0, 1, 3, 2, 4)
        w[nm] = c(a.reshape(2, H, 128, DC * 128))
    a = I["L0_attn_W_G"].reshape(2, DC, 128, H, 128).transpose(0, 3, 2, 1, 4)
    w["wg"] = c(a.reshape(2, H, 128, DC * 128))
    a = I["L0_attn_W_O"].reshape(2, DC, 128, DC, 128).transpose(0, 3, 2, 1, 4)
    w["wo"] = c(a.reshape(2, DC, 128, DC * 128))
    a = I["L0_attn_ffn_w1"].reshape(2, DC, 128, FC, 128).transpose(
        0, 3, 2, 1, 4)
    w["fw1"] = c(a.reshape(2, FC, 128, DC * 128))
    a = I["L0_attn_ffn_w2"].reshape(2, FC, 128, DC, 128).transpose(
        0, 3, 2, 1, 4)
    w["fw2"] = c(a.reshape(2, DC, 128, FC * 128))
    for nm, key, n_in, n_out in (("tw1", "L0_ffn_w1_w", DC, FC),
                                 ("tw2", "L0_ffn_w2_w", DC, FC),
                                 ("th1", "L0_ffn_h1_w", FC, FC),
                                 ("th2", "L0_ffn_h2_w", FC, FC),
                                 ("tw3", "L0_ffn_w3_w", FC, DC)):
        a = np.ascontiguousarray(I[key].T)
        a = a.reshape(n_in, 128, n_out, 128).transpose(2, 1, 0, 3)
        w[nm] = c(a.reshape(n_out, 128, n_in * 128))
    return w


_CACHE = {}


def kernel(**inputs):
    I = {k: np.asarray(v) for k, v in inputs.items()}
    if "nc" not in _CACHE:
        _CACHE["nc"] = build()
    nc = _CACHE["nc"]

    w = host_weights(I)
    pars = host_pars(I)
    cons = host_consts()
    consb = np.eye(128, dtype=BF)
    tabs = [host_tables(0), host_tables(1)]
    X = I["L0_attn_X"]

    in_maps = []
    for c in range(8):
        s, par = c // 2, c % 2
        xs = np.ascontiguousarray(X[s, par * T:(par + 1) * T, :].T,
                                  dtype=np.float32)
        x0 = xs.reshape(DC, 128, QT, QS).transpose(2, 1, 0, 3)
        x0 = np.ascontiguousarray(x0.reshape(QT, 128, DC * QS))
        m = dict(w)
        m.update(x0=x0, xtab=tabs[par], pars=pars, cons=cons, consb=consb,
                 maska=np.full((128, 1), float(par), np.float32))
        in_maps.append(m)

    res = run_bass_kernel_spmd(nc, in_maps, core_ids=list(range(8)))
    out = np.empty((4, 4096, 1024), dtype=np.float32)
    for c in range(8):
        s, par = c // 2, c % 2
        o = np.asarray(res.results[c]["out"])
        o = o.reshape(QT, 128, DC, QS).transpose(2, 1, 0, 3)
        out[s, par * T:(par + 1) * T, :] = o.reshape(1024, T).T
    return out


# version 10
# speedup vs baseline: 1.0500x; 1.0500x over previous
import contextlib
import numpy as np
import ml_dtypes
import concourse.bass as bass
import concourse.mybir as mybir
from concourse.bass_utils import run_bass_kernel_spmd

F32 = mybir.dt.float32
BF16 = mybir.dt.bfloat16
AF = mybir.ActivationFunctionType
ALU = mybir.AluOpType
BF = ml_dtypes.bfloat16

T = 2048
QT = 4
QS = 512
CH = 128
DC = 8
FC = 32
H = 8
EPS = 1e-5
SHUF = [i ^ 1 for i in range(32)]

_PAR = {}
_c = 0
for _nm, _n in [("ln1w", 16), ("ln1b", 16), ("ln2w", 16), ("ln2b", 16),
                ("gnw", 16), ("gnb", 16), ("fb1", 64), ("fb2", 16),
                ("w1s", 32), ("w2s", 32), ("h1s", 32), ("h2s", 32),
                ("w3s", 8)]:
    _PAR[_nm] = _c
    _c += _n
NPAR = _c

CON_MASK = 0
CON_O128 = 128
CON_O1024 = 256
CON_GAM = 384
CON_EPS = 384 + H * 128
CON_NH = CON_EPS + 1
NCON = CON_EPS + 2


_B_DRAINS = [False]


def _toks(waits):
    return [w for w in waits if w is not None]


class B:

    ENGS = ("tensor", "vector", "scalar", "sync", "gpsimd")
    DQ = {"dS": "sync", "dG": "gpsimd", "dA": "scalar"}
    OWNER = {"sync": "dS", "gpsimd": "dG", "scalar": "dA"}

    def __init__(self, nc, sems):
        self.nc, self.sems = nc, sems
        self.val = {k: 0 for k in sems}
        self.hw = {}
        self.mode = None
        self.e = None
        self.scout = False
        self.groups = {s: [0] for s in ("dS", "dG", "dA")}

    def _flush(self):
        for sem, eng in self.DQ.items():
            self.wait(eng, self.tok(sem))

    def run_phase(self, emit):
        ev, eh = dict(self.val), dict(self.hw)
        self.scout, self.mode, self.e = True, None, None
        self.val, self.hw = dict(ev), dict(eh)
        emit(self)
        self._flush()
        self.scout = False
        ret = [None]
        with self.nc.Block() as blk:
            for nm in self.ENGS:
                def body(e, nm=nm):
                    self.val, self.hw = dict(ev), dict(eh)
                    self.mode, self.e = nm, e
                    ret[0] = emit(self)
                    self._flush()
                getattr(blk, nm)(body)
        self.mode, self.e = None, None
        return ret[0]

    def _inc(self, ins, sem, amt):
        if ins is not None:
            ins.then_inc(self.sems[sem], amt)
        self.val[sem] += amt
        return (sem, self.val[sem])

    def tok(self, sem):
        return (sem, self.val[sem])

    def _fix(self, sem, v):
        g = self.groups[sem]
        import bisect
        i = bisect.bisect_left(g, v)
        if i == len(g):
            raise RuntimeError(f"no group end >= {v} for {sem} (have {g[-5:]})")
        return g[i]

    def wait(self, engname, *tokens):
        for t in tokens:
            if t is None:
                continue
            sem, v0 = t
            if v0 <= 0:
                continue
            key = (engname, sem)
            if self.hw.get(key, 0) >= v0:
                continue
            own = self.OWNER.get(engname)
            if self.scout:
                if own is not None:
                    g = self.groups[own]
                    if g[-1] < self.val[own]:
                        g.append(self.val[own])
            else:
                if own is not None:
                    skey = (engname, "_self")
                    if self.val[own] > self.hw.get(skey, 0):
                        if self.mode == engname:
                            self.e.wait_ge(self.sems[own], self.val[own])
                        self.hw[skey] = self.val[own]
                if self.mode == engname:
                    v = self._fix(sem, v0) if sem in self.groups else v0
                    self.e.wait_ge(self.sems[sem], v)
            self.hw[key] = v0

    def snapc(self):
        return [self.tok(s) for s in ("T", "V", "A", "dA", "dG", "dS")]

    def dma(self, q, out, in_, waits=()):
        ename = {"S": "sync", "G": "gpsimd", "A": "scalar"}[q]
        sem = {"S": "dS", "G": "dG", "A": "dA"}[q]
        self.wait(ename, *_toks(waits))
        ins = (self.e.dma_start(out=out, in_=in_)
               if self.mode == ename else None)
        return self._inc(ins, sem, 16)

    def mm(self, out, lhsT, rhs, start, stop, waits=(), sig=False):
        self.wait("tensor", *_toks(waits))
        ins = (self.e.matmul(out, lhsT, rhs, start=start, stop=stop)
               if self.mode == "tensor" else None)
        return self._inc(ins, "T", 1) if sig else None

    def tr(self, out, in_, ident, waits=(), sig=False):
        self.wait("tensor", *_toks(waits))
        ins = (self.e.transpose(out, in_, ident)
               if self.mode == "tensor" else None)
        return self._inc(ins, "T", 1) if sig else None

    def v(self, name, *args, waits=(), sig=False, **kw):
        self.wait("vector", *_toks(waits))
        ins = None
        if self.mode == "vector":
            if _B_DRAINS[0]:
                self.e.drain()
            ins = getattr(self.e, name)(*args, **kw)
        return self._inc(ins, "V", 1) if sig else None

    def a(self, name, *args, waits=(), sig=False, **kw):
        self.wait("scalar", *_toks(waits))
        ins = None
        if self.mode == "scalar":
            if _B_DRAINS[0]:
                self.e.drain()
            ins = getattr(self.e, name)(*args, **kw)
        return self._inc(ins, "A", 1) if sig else None

    def collective(self, kind, op, replica_groups, ins_, outs_, waits=()):
        self.wait("gpsimd", *_toks(waits))
        ins = (self.e.collective_compute(kind, op,
                                         replica_groups=replica_groups,
                                         ins=ins_, outs=outs_)
               if self.mode == "gpsimd" else None)
        return self._inc(ins, "C", 1)

    def split(self, hi, lo, src, waits=(), sig=True):
        tA = self.a("activation", hi, src, AF.Copy, waits=waits, sig=True)
        return self.v("tensor_tensor", lo, src, hi, ALU.subtract,
                      waits=[tA], sig=sig)


def mm3(bb, out, wpair, apair, start, stop, waits=(), sig=False):
    whi, wlo = wpair
    ahi, alo = apair
    bb.mm(out, whi, ahi, start=start, stop=False, waits=waits)
    bb.mm(out, whi, alo, start=False, stop=False)
    return bb.mm(out, wlo, ahi, start=False, stop=stop, sig=sig)


def Q(i):
    return slice(i * QS, (i + 1) * QS)


def C1(i):
    return slice(i * 128, (i + 1) * 128)


def build(debug=False, stop=None, racecheck=False):
    nc = bass.Bass("TRN2", target_bir_lowering=False,
                   detect_race_conditions=racecheck)
    _B_DRAINS[0] = racecheck

    def din(n, s, d=F32):
        return nc.dram_tensor(n, s, d, kind="ExternalInput")

    def dscr(n, s, d=F32):
        if debug:
            return nc.dram_tensor(n, s, d, kind="ExternalOutput")
        return nc.dram_tensor(n, s, d)

    D = {}
    D["x0"] = din("x0", [QT, 128, DC * QS])
    D["wq"] = din("wq", [2, H, 128, DC * 128])
    D["wk"] = din("wk", [2, H, 128, DC * 128])
    D["wv"] = din("wv", [2, H, 128, DC * 128])
    D["wg"] = din("wg", [2, H, 128, DC * 128])
    D["wo"] = din("wo", [2, DC, 128, DC * 128])
    D["fw1"] = din("fw1", [2, FC, 128, DC * 128])
    D["fw2"] = din("fw2", [2, DC, 128, FC * 128])
    D["tw1"] = din("tw1", [FC, 128, DC * 128])
    D["tw2"] = din("tw2", [FC, 128, DC * 128])
    D["th1"] = din("th1", [FC, 128, FC * 128])
    D["th2"] = din("th2", [FC, 128, FC * 128])
    D["tw3"] = din("tw3", [DC, 128, FC * 128])
    D["xtab"] = din("xtab", [H, QT, 128, 4 * QS])
    D["pars"] = din("pars", [128, NPAR])
    D["cons"] = din("cons", [128, NCON])
    D["consb"] = din("consb", [128, 128], BF16)
    D["maska"] = din("maska", [128, 1])
    D["out"] = nc.dram_tensor("out", [QT, 128, DC * QS], F32,
                              kind="ExternalOutput")
    D["sc_y"] = dscr("sc_y", [QT, 128, DC * QS])
    D["sc_x"] = dscr("sc_x", [QT, 128, DC * QS])
    D["sc_x1"] = dscr("sc_x1", [QT, 128, DC * QS])
    D["sc_ln1"] = dscr("sc_ln1", [QT, 2, 128, DC * QS], BF16)
    D["sc_kv"] = dscr("sc_kv", [H, QT, 6, 128, QS], BF16)
    D["sc_w1"] = dscr("sc_w1", [2, FC, 2, 128, DC * 128], BF16)
    D["sc_w2"] = dscr("sc_w2", [2, DC, 2, 128, FC * 128], BF16)
    D["cc_in"] = dscr("cc_in", [128, H * 128])
    D["cc_out"] = dscr("cc_out", [256, H * 128])

    with contextlib.ExitStack() as ST:
        sems = {n: ST.enter_context(nc.semaphore(n))
                for n in ("dS", "dG", "dA", "T", "V", "A", "C")}
        bb = B(nc, sems)

        sbp = lambda n, s, d=F32: ST.enter_context(nc.sbuf_tensor(n, s, d))
        G = {}
        G["c_par"] = sbp("c_par", [128, NPAR])
        G["c_con"] = sbp("c_con", [128, NCON])
        G["c_idb"] = sbp("c_idb", [128, 128], BF16)
        G["c_ma"] = sbp("c_ma", [128, 1])
        G["sq"] = sbp("sq", [128, 2 * QS])
        G["mu"] = sbp("mu", [128, QS])
        G["msq"] = sbp("msq", [128, QS])
        G["var"] = sbp("var", [128, QS])
        G["rstd"] = sbp("rstd", [128, QS])
        G["lnf"] = sbp("lnf", [128, QS])
        G["pb"] = [ST.enter_context(nc.psum_tensor(f"pb{i}", [128, 512], F32))
                   for i in range(7)]
        G["pt"] = ST.enter_context(nc.psum_tensor("pt", [128, 1024], BF16))

        cc = G["c_con"]
        G["MASK"] = cc[:, CON_MASK:CON_MASK + 128]
        G["O128"] = cc[:, CON_O128:CON_O128 + 128]
        G["O1024"] = cc[:, CON_O1024:CON_O1024 + 128]
        G["GAM"] = lambda h: cc[:, CON_GAM + h * 128:CON_GAM + (h + 1) * 128]
        G["P1"] = lambda nm, i: G["c_par"][:, _PAR[nm] + i:_PAR[nm] + i + 1]
        G["IDB"] = G["c_idb"][:, :]
        G["EPSA"] = cc[:, CON_EPS:CON_EPS + 1]
        G["NHA"] = cc[:, CON_NH:CON_NH + 1]

        def emit_boot(bb):
            bb.dma("S", G["c_par"][:, :], D["pars"][:, :])
            bb.dma("S", G["c_con"][:, :], D["cons"][:, :])
            bb.dma("S", G["c_idb"][:, :], D["consb"][:, :])
            return bb.dma("S", G["c_ma"][:, :], D["maska"][:, :])

        G["boot"] = bb.run_phase(emit_boot)

        def ln_stats(src, waits=()):
            pb = G["pb"]
            sq, mu, msq, var, rstd = (G[k] for k in
                                      ("sq", "mu", "msq", "var", "rstd"))
            tsq = [None, None]
            for dc in range(DC):
                xs = src[:, Q(dc)]
                w0 = list(waits) if dc == 0 else []
                tA = bb.a("activation", sq[:, Q(dc % 2)], xs, AF.Square,
                          waits=w0 + [tsq[dc % 2]], sig=True)
                bb.mm(pb[0][:, :], G["O1024"], xs, start=(dc == 0),
                      stop=(dc == DC - 1), waits=w0)
                tsq[dc % 2] = bb.mm(pb[1][:, :], G["O1024"], sq[:, Q(dc % 2)],
                                    start=(dc == 0), stop=(dc == DC - 1),
                                    waits=[tA], sig=True)
            tst = bb.tok("T")
            bb.v("tensor_copy", mu[:, :], pb[0][:, :], waits=[tst])
            bb.v("tensor_tensor", msq[:, :], mu[:, :], mu[:, :], ALU.mult)
            tv = bb.v("tensor_tensor", var[:, :], pb[1][:, :], msq[:, :],
                      ALU.subtract, sig=True)
            bb.a("activation", rstd[:, :], var[:, :], AF.Ln,
                 bias=G["EPSA"], waits=[tv])
            return bb.a("activation", rstd[:, :], rstd[:, :], AF.Exp,
                        scale=G["NHA"], sig=True)

        G["ln_stats"] = ln_stats
        lay_src = [D["x0"], D["sc_x"]]

        done = False
        with contextlib.ExitStack() as SLY:
            sly = lambda n, s, d=F32: SLY.enter_context(
                nc.sbuf_tensor(n, s, d))
            G["s_slab"] = sly("s_slab", [128, H * 128])
            G["s_hi"] = sly("s_hi", [128, 128], BF16)
            G["s_lo"] = sly("s_lo", [128, 128], BF16)
            G["srecv"] = sly("srecv", [128, H * 128])
            G["bias_wo"] = sly("bias_wo", [128, DC])
            for L in range(2):
                attn_layer(nc, bb, D, G, L, lay_src, stop)
                if stop in (f"L{L}", f"R1L{L}", f"R2L{L}"):
                    done = True
                    break
        if not done:
            tail(nc, bb, D, G)

        def emit_fin(bb):
            bb.wait("sync", bb.tok("dS"))
            bb.wait("gpsimd", bb.tok("dG"))
            bb.wait("scalar", bb.tok("dA"))

        bb.run_phase(emit_fin)
    return nc


def attn_layer(nc, bb, D, G, L, lay_src, stop=None):
    pb, pt = G["pb"], G["pt"]
    P1, MASK, O128, GAM, IDB = (G[k] for k in
                                ("P1", "MASK", "O128", "GAM", "IDB"))
    s_slab, s_hi, s_lo, srecv = (G[k] for k in
                                 ("s_slab", "s_hi", "s_lo", "srecv"))
    mu, msq, lnf, rstd = (G[k] for k in ("mu", "msq", "lnf", "rstd"))
    ln_stats = G["ln_stats"]
    boot = G["boot"]

    with contextlib.ExitStack() as S1:
        sb = lambda n, s, d=F32: S1.enter_context(
            nc.sbuf_tensor(n + str(L), s, d))
        wkh = sb("wkh", [128, H * DC * 128], BF16)
        wkl = sb("wkl", [128, H * DC * 128], BF16)
        wvh = sb("wvh", [128, H * DC * 128], BF16)
        wvl = sb("wvl", [128, H * DC * 128], BF16)
        wland = sb("wland", [128, 2 * DC * 128])
        xt = sb("xt", [128, DC * QS])
        l1h = sb("l1h", [128, DC * QS], BF16)
        l1l = sb("l1l", [128, DC * QS], BF16)
        ttab = sb("ttab", [128, 2 * 2 * QS])
        kfh = sb("kfh", [128, 2 * QS], BF16)
        kfl = sb("kfl", [128, 2 * QS], BF16)
        vfh = sb("vfh", [128, 2 * QS], BF16)
        vfl = sb("vfl", [128, 2 * QS], BF16)
        kth = sb("kth", [128, 2 * QS], BF16)
        ktl = sb("ktl", [128, 2 * QS], BF16)
        vth = sb("vth", [128, 2 * QS], BF16)
        vtl = sb("vtl", [128, 2 * QS], BF16)
        scA = sb("scA", [128, 2 * QS])
        scB = sb("scB", [128, 2 * QS])
        f1l = sb("f1l", [128, DC * 128])
        f1h = sb("f1h", [128, DC * 128], BF16)
        f1lo = sb("f1lo", [128, DC * 128], BF16)
        f2l = sb("f2l", [128, FC * 64])
        f2h = sb("f2h", [128, FC * 64], BF16)
        f2lo = sb("f2lo", [128, FC * 64], BF16)

        def emit_r1(bb):
            ent = [bb.tok("T"), bb.tok("V"), bb.tok("A")]
            bb.wait("sync", *ent)
            bb.wait("scalar", *ent)
            bb.wait("scalar", bb.tok("dG"))
            bb.wait("gpsimd", *ent)

            wtok = [None, None]
            for i, (key, dh, dl) in enumerate([("wk", wkh, wkl),
                                               ("wv", wvh, wvl)]):
                for h in range(H):
                    p = (i * H + h) % 2
                    sl = slice(p * DC * 128, (p + 1) * DC * 128)
                    dsl = slice(h * DC * 128, (h + 1) * DC * 128)
                    tl = bb.dma("S", wland[:, sl], D[key][L, h, :, :],
                                waits=[wtok[p], boot])
                    wtok[p] = bb.split(dh[:, dsl], dl[:, dsl], wland[:, sl],
                                       waits=[tl])
            w_r1 = bb.tok("V")

            jobs = []
            jst = {"f1": [None], "f2": [None]}

            def mk_job(kind, oc, half=0):
                def go():
                    if kind == "f1":
                        land, hi, lo = f1l, f1h, f1lo
                        src = D["fw1"][L, oc, :, :]
                        dsth = D["sc_w1"][L, oc, 0, :, :]
                        dstl = D["sc_w1"][L, oc, 1, :, :]
                    else:
                        n = FC * 64
                        land, hi, lo = f2l, f2h, f2lo
                        src = D["fw2"][L, oc, :, half * n:(half + 1) * n]
                        dsth = D["sc_w2"][L, oc, 0, :, half * n:(half + 1) * n]
                        dstl = D["sc_w2"][L, oc, 1, :, half * n:(half + 1) * n]
                    tl = bb.dma("S", land[:, :], src, waits=[jst[kind][0]])
                    ts = bb.split(hi[:, :], lo[:, :], land[:, :], waits=[tl])
                    bb.dma("G", dsth, hi[:, :], waits=[ts])
                    jst[kind][0] = bb.dma("G", dstl, lo[:, :])
                return go

            for oc in range(FC):
                jobs.append(mk_job("f1", oc))
            for oc in range(DC):
                jobs.append(mk_job("f2", oc, 0))
                jobs.append(mk_job("f2", oc, 1))

            bb.v("memset", s_slab[:, :], 0.0)

            itok = {}
            gh = 0
            prev_tks = prev_tvs = prev_tve = prev_tSm = None
            for qt in range(QT):
                txw = itok.get(gh - 1, []) + [boot] + (ent if gh == 0 else [])
                tx = bb.dma("A", xt[:, :], lay_src[L][qt, :, :], waits=txw)
                trs = ln_stats(xt, waits=[tx])
                for dc in range(DC):
                    bb.v("tensor_tensor", scA[:, Q(0)], xt[:, Q(dc)], mu[:, :],
                         ALU.subtract, waits=[trs] if dc == 0 else [])
                    bb.v("tensor_tensor", scA[:, Q(0)], scA[:, Q(0)], rstd[:, :],
                         ALU.mult)
                    tln = bb.v("tensor_scalar", lnf[:, :], scA[:, Q(0)],
                               P1("ln1w", L * DC + dc),
                               P1("ln1b", L * DC + dc),
                               ALU.mult, ALU.add, sig=True)
                    bb.split(l1h[:, Q(dc)], l1l[:, Q(dc)], lnf[:, :],
                             waits=[tln])
                tl1 = bb.tok("V")
                bb.dma("G", D["sc_ln1"][qt, 0, :, :], l1h[:, :], waits=[tl1])
                bb.dma("G", D["sc_ln1"][qt, 1, :, :], l1l[:, :])

                for h in range(H):
                    p = h % 2
                    pw = itok.get(gh - 2, [])
                    bb.wait("tensor", *pw)
                    bb.wait("vector", *pw)
                    if jobs:
                        jobs.pop(0)()
                    psl = slice(p * 2 * QS, (p + 1) * 2 * QS)
                    ttb = bb.dma("A", ttab[:, psl],
                                 D["xtab"][h, qt, :, 2 * QS:4 * QS], waits=pw)
                    TAB = lambda i: ttab[:, p * 2 * QS + (i - 2) * QS:
                                         p * 2 * QS + (i - 1) * QS]
                    PP = lambda buf: buf[:, Q(p)]
                    KS = lambda buf, cci: buf[:, p * QS + cci * 128:
                                              p * QS + (cci + 1) * 128]

                    tK = None
                    for dc in range(DC):
                        hh = slice(h * DC * 128 + dc * 128,
                                   h * DC * 128 + (dc + 1) * 128)
                        tK = mm3(bb, pb[2][:, :], (wkh[:, hh], wkl[:, hh]),
                                 (l1h[:, Q(dc)], l1l[:, Q(dc)]),
                                 start=(dc == 0), stop=(dc == DC - 1),
                                 waits=[w_r1, tl1, prev_tks] if dc == 0 else [],
                                 sig=(dc == DC - 1))
                    bb.v("tensor_tensor", PP(scA), pb[2][:, :], TAB(2), ALU.mult,
                         waits=[tK, ttb])
                    bb.v("tensor_copy", PP(scB), pb[2][:, :])
                    bb.v("stream_shuffle", lnf[:, :], PP(scB), SHUF)
                    bb.v("tensor_tensor", lnf[:, :], lnf[:, :], TAB(3), ALU.mult)
                    txp = bb.v("tensor_tensor", PP(scA), PP(scA), lnf[:, :],
                               ALU.add, sig=True)
                    tks = bb.split(PP(kfh), PP(kfl), PP(scA), waits=[txp])
                    tV = None
                    for dc in range(DC):
                        hh = slice(h * DC * 128 + dc * 128,
                                   h * DC * 128 + (dc + 1) * 128)
                        tV = mm3(bb, pb[3][:, :], (wvh[:, hh], wvl[:, hh]),
                                 (l1h[:, Q(dc)], l1l[:, Q(dc)]),
                                 start=(dc == 0), stop=(dc == DC - 1),
                                 waits=[prev_tvs] if dc == 0 else [],
                                 sig=(dc == DC - 1))
                    tvc = bb.v("tensor_copy", PP(scB), pb[3][:, :],
                               waits=[tV], sig=True)
                    tvs = bb.split(PP(vfh), PP(vfl), PP(scB), waits=[tvc])
                    for cci in range(4):
                        bb.tr(pt[:, C1(cci)], KS(kfh, cci), IDB,
                              waits=[tks, prev_tve] if cci == 0 else [])
                        bb.tr(pt[:, C1(4 + cci)], KS(kfl, cci), IDB,
                              sig=(cci == 3))
                    tt1 = bb.tok("T")
                    bb.v("tensor_copy", PP(kth), pt[:, 0:QS], waits=[tt1])
                    tke = bb.v("tensor_copy", PP(ktl), pt[:, QS:2 * QS],
                               sig=True)
                    for cci in range(4):
                        bb.tr(pt[:, C1(cci)], KS(vfh, cci), IDB,
                              waits=[tvs, tke] if cci == 0 else [])
                        bb.tr(pt[:, C1(4 + cci)], KS(vfl, cci), IDB,
                              sig=(cci == 3))
                    tt2 = bb.tok("T")
                    bb.v("tensor_copy", PP(vth), pt[:, 0:QS], waits=[tt2])
                    tve = bb.v("tensor_copy", PP(vtl), pt[:, QS:2 * QS],
                               sig=True)
                    for cci in range(4):
                        bb.mm(pb[5][:, 0:128], KS(kth, cci), KS(vth, cci),
                              start=True, stop=False,
                              waits=[tve, prev_tSm] if cci == 0 else [prev_tSm])
                        bb.mm(pb[5][:, 0:128], KS(ktl, cci), KS(vth, cci),
                              start=False, stop=False)
                        tS = bb.mm(pb[5][:, 0:128], KS(kth, cci), KS(vtl, cci),
                                   start=False, stop=True, sig=True)
                        bb.v("tensor_tensor", s_slab[:, C1(h)], s_slab[:, C1(h)],
                             pb[5][:, 0:128], ALU.add, waits=[tS])
                        prev_tSm = bb.v("tensor_tensor", s_slab[:, C1(h)],
                                        s_slab[:, C1(h)], GAM(h), ALU.mult,
                                        sig=True)
                    t_all = [bb.tok("V"), bb.tok("A")]
                    bb.dma("G", D["sc_kv"][h, qt, 0, :, :], PP(kfh),
                           waits=t_all)
                    bb.dma("G", D["sc_kv"][h, qt, 1, :, :], PP(kfl))
                    bb.dma("G", D["sc_kv"][h, qt, 2, :, :], PP(kth))
                    bb.dma("G", D["sc_kv"][h, qt, 3, :, :], PP(ktl))
                    bb.dma("G", D["sc_kv"][h, qt, 4, :, :], PP(vth))
                    bb.dma("G", D["sc_kv"][h, qt, 5, :, :], PP(vtl))
                    prev_tks, prev_tvs, prev_tve = tks, tvs, tve
                    itok[gh] = bb.snapc()
                    gh += 1
            while jobs:
                jobs.pop(0)()

            tSfin = bb.tok("V")
            bb.dma("G", D["cc_in"][:, :], s_slab[:, :], waits=[tSfin])
            bb.wait("gpsimd", bb.tok("dG"))
            tcc = bb.collective(
                "AllGather", ALU.bypass,
                [[0, 1], [2, 3], [4, 5], [6, 7]],
                [D["cc_in"].ap().opt()], [D["cc_out"].ap().opt()])
            trecv = bb.dma("G", srecv[:, :], D["cc_out"][0:128, :], waits=[tcc])
            return bb.v("tensor_scalar", s_slab[:, :], srecv[:, :],
                        G["c_ma"][:, 0:1], None, ALU.mult, waits=[trecv],
                        sig=True)

        tinit = bb.run_phase(emit_r1)

    if stop == f"R1L{L}":
        return
    with contextlib.ExitStack() as S2:
        sb = lambda n, s, d=F32: S2.enter_context(
            nc.sbuf_tensor(n + "r2" + str(L), s, d))
        wqh = sb("wqh", [128, H * DC * 128], BF16)
        wql = sb("wql", [128, H * DC * 128], BF16)
        wgh = sb("wgh", [128, 2 * DC * 128], BF16)
        wgl = sb("wgl", [128, 2 * DC * 128], BF16)
        wland = sb("wland", [128, 2 * DC * 128])
        l1h = sb("l1h", [128, DC * QS], BF16)
        l1l = sb("l1l", [128, DC * QS], BF16)
        ttab = sb("ttab", [128, 2 * 2 * QS])
        kfh = sb("kfh", [128, 2 * QS], BF16)
        kfl = sb("kfl", [128, 2 * QS], BF16)
        kth = sb("kth", [128, 2 * QS], BF16)
        ktl = sb("ktl", [128, 2 * QS], BF16)
        vth = sb("vth", [128, 2 * QS], BF16)
        vtl = sb("vtl", [128, 2 * QS], BF16)
        qph = sb("qph", [128, 2 * QS], BF16)
        qpl = sb("qpl", [128, 2 * QS], BF16)
        sch = sb("sch", [128, 4 * 128], BF16)
        scl = sb("scl", [128, 4 * 128], BF16)
        scA = sb("scA", [128, 2 * QS])
        scB = sb("scB", [128, 2 * QS])
        ymu = sb("ymu", [128, H * QS])
        vst = sb("vst", [128, H * QS])
        gyh = sb("gyh", [128, H * QS], BF16)
        gyl = sb("gyl", [128, H * QS], BF16)
        wos = sb("wos", [128, DC * 128])
        wohi = sb("wohi", [128, DC * 128], BF16)
        wolo = sb("wolo", [128, DC * 128], BF16)
        woev = sb("woev", [128, QS])
        xt2 = sb("xt2", [128, 2 * QS])
        woland = wland

        def emit_r2(bb):
            ent = [bb.tok("T"), bb.tok("V"), bb.tok("A")]
            bb.wait("sync", *ent)
            bb.wait("scalar", *ent)
            bb.wait("scalar", bb.tok("dG"))
            bb.wait("gpsimd", *ent)

            wl_tok = [None, None]
            for h in range(H):
                p = h % 2
                sl = slice(p * DC * 128, (p + 1) * DC * 128)
                dsl = slice(h * DC * 128, (h + 1) * DC * 128)
                tl = bb.dma("S", wland[:, sl], D["wq"][L, h, :, :],
                            waits=[wl_tok[p]])
                wl_tok[p] = bb.split(wqh[:, dsl], wql[:, dsl], wland[:, sl],
                                     waits=[tl])
            w_r2 = bb.tok("V")

            itok = {}
            gh = 0
            last_y = None
            prev_tSm = None
            prev_tqs = None
            prev_tspl = [None] * 4
            prev_gnb = None
            prev_tvst = None
            prev_wo_ev = [None, None]
            wg_cons = [None, None]
            wl_bias = [None, None]
            for qt in range(QT):
                pv = itok.get(gh - 1, []) + (ent if gh == 0 else [])
                tlh = bb.dma("A", l1h[:, :], D["sc_ln1"][qt, 0, :, :], waits=pv)
                tll = bb.dma("A", l1l[:, :], D["sc_ln1"][qt, 1, :, :])
                for h in range(H):
                    p = h % 2
                    pw = itok.get(gh - 2, [])
                    bb.wait("tensor", *pw)
                    bb.wait("vector", *pw)
                    psl = slice(p * 2 * QS, (p + 1) * 2 * QS)
                    ttb = bb.dma("A", ttab[:, psl],
                                 D["xtab"][h, qt, :, 0:2 * QS], waits=pw)
                    bb.dma("A", kfh[:, Q(p)], D["sc_kv"][h, qt, 0, :, :])
                    bb.dma("A", kfl[:, Q(p)], D["sc_kv"][h, qt, 1, :, :])
                    bb.dma("A", kth[:, Q(p)], D["sc_kv"][h, qt, 2, :, :])
                    bb.dma("A", ktl[:, Q(p)], D["sc_kv"][h, qt, 3, :, :])
                    bb.dma("A", vth[:, Q(p)], D["sc_kv"][h, qt, 4, :, :])
                    tk5 = bb.dma("A", vtl[:, Q(p)], D["sc_kv"][h, qt, 5, :, :])
                    TAB = lambda i: ttab[:, p * 2 * QS + i * QS:
                                         p * 2 * QS + (i + 1) * QS]
                    PP = lambda buf: buf[:, Q(p)]
                    KS = lambda buf, cci: buf[:, p * QS + cci * 128:
                                              p * QS + (cci + 1) * 128]

                    tQ = None
                    for dc in range(DC):
                        hh = slice(h * DC * 128 + dc * 128,
                                   h * DC * 128 + (dc + 1) * 128)
                        tQ = mm3(bb, pb[2][:, :], (wqh[:, hh], wql[:, hh]),
                                 (l1h[:, Q(dc)], l1l[:, Q(dc)]),
                                 start=(dc == 0), stop=(dc == DC - 1),
                                 waits=[w_r2, tlh, tll, prev_tqs]
                                 if dc == 0 else [], sig=(dc == DC - 1))
                    bb.v("tensor_tensor", PP(scA), pb[2][:, :], TAB(0), ALU.mult,
                         waits=[tQ, ttb])
                    bb.v("tensor_copy", PP(scB), pb[2][:, :])
                    bb.v("stream_shuffle", lnf[:, :], PP(scB), SHUF)
                    bb.v("tensor_tensor", lnf[:, :], lnf[:, :], TAB(1), ALU.mult)
                    txp = bb.v("tensor_tensor", PP(scA), PP(scA), lnf[:, :],
                               ALU.add, sig=True)
                    tqs = bb.split(PP(qph), PP(qpl), PP(scA), waits=[txp])
                    prev_tqs = tqs

                    for cci in range(4):
                        if cci == 0:
                            bb.split(s_hi[:, :], s_lo[:, :], s_slab[:, C1(h)],
                                     waits=[last_y, tinit, prev_tSm])
                        bb.mm(pb[4][:, C1(cci)], KS(kfh, cci), KS(qph, cci),
                              start=True, stop=False,
                              waits=[tk5, tqs, prev_tspl[cci]]
                              if cci == 0 else [prev_tspl[cci]])
                        bb.mm(pb[4][:, C1(cci)], KS(kfh, cci), KS(qpl, cci),
                              start=False, stop=False)
                        tsc = bb.mm(pb[4][:, C1(cci)], KS(kfl, cci),
                                    KS(qph, cci), start=False, stop=True,
                                    sig=True)
                        tmk = bb.v("tensor_tensor",
                                   scB[:, p * QS:p * QS + 128],
                                   pb[4][:, C1(cci)], MASK, ALU.mult,
                                   waits=[tsc], sig=True)
                        tspl = bb.split(sch[:, C1(cci)], scl[:, C1(cci)],
                                        scB[:, p * QS:p * QS + 128],
                                        waits=[tmk])
                        prev_tspl[cci] = tspl
                        bb.mm(pb[5][:, C1(cci)], KS(vth, cci), sch[:, C1(cci)],
                              start=True, stop=False, waits=[tspl])
                        bb.mm(pb[5][:, C1(cci)], KS(vth, cci), scl[:, C1(cci)],
                              start=False, stop=False)
                        bb.mm(pb[5][:, C1(cci)], KS(vtl, cci), sch[:, C1(cci)],
                              start=False, stop=False)
                        bb.mm(pb[5][:, C1(cci)], s_hi[:, :], KS(qph, cci),
                              start=False, stop=False)
                        bb.mm(pb[5][:, C1(cci)], s_hi[:, :], KS(qpl, cci),
                              start=False, stop=False)
                        last_y = bb.mm(pb[5][:, C1(cci)], s_lo[:, :],
                                       KS(qph, cci), start=False, stop=True,
                                       sig=True)
                        bb.mm(pb[6][:, 0:128], KS(kth, cci), KS(vth, cci),
                              start=True, stop=False)
                        bb.mm(pb[6][:, 0:128], KS(ktl, cci), KS(vth, cci),
                              start=False, stop=False)
                        tS = bb.mm(pb[6][:, 0:128], KS(kth, cci), KS(vtl, cci),
                                   start=False, stop=True, sig=True)
                        bb.v("tensor_tensor", s_slab[:, C1(h)],
                             s_slab[:, C1(h)], pb[6][:, 0:128], ALU.add,
                             waits=[tS])
                        tSm = bb.v("tensor_tensor", s_slab[:, C1(h)],
                                   s_slab[:, C1(h)], GAM(h), ALU.mult,
                                   sig=True)
                        if cci < 3:
                            bb.split(s_hi[:, :], s_lo[:, :], s_slab[:, C1(h)],
                                     waits=[last_y, tSm])
                    prev_tSm = tSm
                    tyf = bb.v("tensor_copy", PP(scA), pb[5][:, :],
                               waits=[last_y], sig=True)
                    bb.mm(pb[0][:, :], O128, PP(scA), start=True, stop=True,
                          waits=[tyf, prev_gnb] + prev_wo_ev)
                    ty2 = bb.a("activation", PP(scB), PP(scA), AF.Square,
                               waits=[tyf], sig=True)
                    tm2 = bb.mm(pb[1][:, :], O128, PP(scB), start=True,
                                stop=True, waits=[ty2, prev_tvst], sig=True)
                    bb.v("tensor_copy", mu[:, :], pb[0][:, :], waits=[tm2])
                    prev_gnb = bb.v("tensor_tensor",
                                    ymu[:, h * QS:(h + 1) * QS], PP(scA),
                                    mu[:, :], ALU.subtract, sig=True)
                    bb.v("tensor_tensor", msq[:, :], mu[:, :], mu[:, :],
                         ALU.mult)
                    prev_tvst = bb.v("tensor_tensor",
                                     vst[:, h * QS:(h + 1) * QS],
                                     pb[1][:, :], msq[:, :], ALU.subtract,
                                     sig=True)
                    itok[gh] = bb.snapc()
                    gh += 1
                tv1 = bb.tok("V")
                bb.a("activation", vst[:, :], vst[:, :], AF.Ln,
                     bias=G["EPSA"], waits=[tv1])
                trstd = bb.a("activation", vst[:, :], vst[:, :], AF.Exp,
                             scale=G["NHA"], sig=True)
                tgp = None
                et3 = {}
                for h in range(H):
                    p = h % 2
                    wsl = slice(p * DC * 128, (p + 1) * DC * 128)
                    tl = bb.dma("S", wland[:, wsl], D["wg"][L, h, :, :],
                                waits=[wl_tok[p], wl_bias[p]])
                    wl_tok[p] = tgs = bb.split(wgh[:, wsl], wgl[:, wsl],
                                               wland[:, wsl],
                                               waits=[tl, wg_cons[p]])
                    tG = None
                    for dc in range(DC):
                        gsl = slice(p * DC * 128 + dc * 128,
                                    p * DC * 128 + (dc + 1) * 128)
                        tG = mm3(bb, pb[2][:, :], (wgh[:, gsl], wgl[:, gsl]),
                                 (l1h[:, Q(dc)], l1l[:, Q(dc)]),
                                 start=(dc == 0), stop=(dc == DC - 1),
                                 waits=[tgp if tgp else trstd, tgs]
                                 if dc == 0 else [], sig=(dc == DC - 1))
                    wg_cons[p] = tG
                    tsil = bb.a("activation", scA[:, Q(p)], pb[2][:, :], AF.Silu,
                                waits=[tG, et3.get(h - 2)], sig=True)
                    tgp = tsil
                    hs = slice(h * QS, (h + 1) * QS)
                    bb.v("tensor_tensor", scB[:, Q(0)], ymu[:, hs], vst[:, hs],
                         ALU.mult, waits=[trstd, tsil])
                    tgm = bb.v("tensor_tensor", scB[:, Q(0)], scB[:, Q(0)],
                               scA[:, Q(p)], ALU.mult, sig=True)
                    et3[h] = bb.split(gyh[:, hs], gyl[:, hs], scB[:, Q(0)],
                                      waits=[tgm])
                tgy = bb.tok("V")
                prev_two = None
                for oc in range(DC):
                    p = oc % 2
                    wsl = slice(p * DC * 128, (p + 1) * DC * 128)
                    tl = bb.dma("S", wland[:, wsl], D["wo"][L, oc, :, :],
                                waits=[wl_tok[p], wl_bias[p]])
                    if qt == 0:
                        tb = None
                        for vc in range(DC):
                            tb = bb.mm(pb[3][:, oc:oc + 1],
                                       wland[:, p * DC * 128 + vc * 128:
                                             p * DC * 128 + (vc + 1) * 128],
                                       P1("gnb", L * DC + vc),
                                       start=(vc == 0), stop=(vc == DC - 1),
                                       waits=[tl, tgy] if vc == 0 else [],
                                       sig=(vc == DC - 1))
                        wl_bias[p] = tb
                        bb.v("tensor_copy", G["bias_wo"][:, oc:oc + 1],
                             pb[3][:, oc:oc + 1], waits=[tb])
                    tsw = None
                    for vc in range(DC):
                        tsw = bb.v("tensor_scalar", wos[:, C1(vc)],
                                   wland[:, p * DC * 128 + vc * 128:
                                         p * DC * 128 + (vc + 1) * 128],
                                   P1("gnw", L * DC + vc), None, ALU.mult,
                                   waits=[tl, prev_two] if vc == 0 else [],
                                   sig=(vc == DC - 1))
                    tws = bb.split(wohi[:, :], wolo[:, :], wos[:, :],
                                   waits=[tsw])
                    wl_tok[p] = tws
                    pbank = pb[0] if oc % 2 == 0 else pb[1]
                    txr = bb.dma("A", xt2[:, Q(p)], lay_src[L][qt, :, Q(oc)],
                                 waits=[prev_wo_ev[p]])
                    two = None
                    for vc in range(DC):
                        two = mm3(bb, pbank[:, :],
                                  (wohi[:, C1(vc)], wolo[:, C1(vc)]),
                                  (gyh[:, vc * QS:(vc + 1) * QS],
                                   gyl[:, vc * QS:(vc + 1) * QS]),
                                  start=(vc == 0), stop=(vc == DC - 1),
                                  waits=[tws, tgy, prev_wo_ev[p]]
                                  if vc == 0 else [], sig=(vc == DC - 1))
                    prev_two = two
                    bb.v("tensor_scalar", woev[:, :], pbank[:, :],
                         G["bias_wo"][:, oc:oc + 1], None, ALU.add,
                         waits=[two, txr])
                    tev = bb.v("tensor_tensor", xt2[:, Q(p)], woev[:, :],
                               xt2[:, Q(p)], ALU.add, sig=True)
                    prev_wo_ev[p] = bb.dma("G", D["sc_y"][qt, :, Q(oc)],
                                           xt2[:, Q(p)], waits=[tev])
                itok[gh - 1] = bb.snapc()

        bb.run_phase(emit_r2)

    if stop == f"R2L{L}":
        return
    with contextlib.ExitStack() as S3:
        sb = lambda n, s, d=F32: S3.enter_context(
            nc.sbuf_tensor(n + "ffn" + str(L), s, d))
        yt = sb("yt", [128, DC * QS])
        l2h = sb("l2h", [128, DC * QS], BF16)
        l2l = sb("l2l", [128, DC * QS], BF16)
        w1h = sb("w1h", [128, 2 * DC * 128], BF16)
        w1l = sb("w1l", [128, 2 * DC * 128], BF16)
        w2h = sb("w2h", [128, FC * 128], BF16)
        w2l = sb("w2l", [128, FC * 128], BF16)
        hh_ = sb("hh", [128, FC * QS], BF16)
        hl_ = sb("hl", [128, FC * QS], BF16)
        gsc = sb("gsc", [128, 2 * QS])
        esc = sb("esc", [128, QS])

        def emit_ffn(bb):
            ent = [bb.tok("T"), bb.tok("V"), bb.tok("A")]
            bb.wait("sync", *ent)
            bb.wait("sync", bb.tok("dG"))
            bb.wait("scalar", *ent)
            bb.wait("scalar", bb.tok("dG"))
            bb.wait("gpsimd", *ent)

            ln_stats = G["ln_stats"]
            prev = []
            for qt in range(QT):
                ty = bb.dma("A", yt[:, :], D["sc_y"][qt, :, :],
                            waits=prev + (ent if qt == 0 else []))
                trs = ln_stats(yt, waits=[ty])
                for dc in range(DC):
                    bb.v("tensor_tensor", gsc[:, Q(0)], yt[:, Q(dc)], mu[:, :],
                         ALU.subtract, waits=[trs] if dc == 0 else [])
                    bb.v("tensor_tensor", gsc[:, Q(0)], gsc[:, Q(0)], rstd[:, :],
                         ALU.mult)
                    tln = bb.v("tensor_scalar", lnf[:, :], gsc[:, Q(0)],
                               P1("ln2w", L * DC + dc),
                               P1("ln2b", L * DC + dc),
                               ALU.mult, ALU.add, sig=True)
                    bb.split(l2h[:, Q(dc)], l2l[:, Q(dc)], lnf[:, :],
                             waits=[tln])
                tl2 = bb.tok("V")
                wtok = [None, None]
                etok = {}
                for oc in range(FC):
                    p = oc % 2
                    wsl = slice(p * DC * 128, (p + 1) * DC * 128)
                    bb.dma("S", w1h[:, wsl], D["sc_w1"][L, oc, 0, :, :],
                           waits=[wtok[p]])
                    twl = bb.dma("S", w1l[:, wsl], D["sc_w1"][L, oc, 1, :, :])
                    pbank = pb[2 + (oc % 4)]
                    tmm = None
                    for dc in range(DC):
                        tmm = mm3(bb, pbank[:, :],
                                  (w1h[:, p * DC * 128 + dc * 128:
                                       p * DC * 128 + (dc + 1) * 128],
                                   w1l[:, p * DC * 128 + dc * 128:
                                       p * DC * 128 + (dc + 1) * 128]),
                                  (l2h[:, Q(dc)], l2l[:, Q(dc)]),
                                  start=(dc == 0), stop=(dc == DC - 1),
                                  waits=[twl, tl2, etok.get(oc - 4)]
                                  if dc == 0 else [], sig=(dc == DC - 1))
                    wtok[p] = tmm
                    tg = bb.a("activation", gsc[:, Q(oc % 2)], pbank[:, :],
                              AF.Gelu, bias=P1("fb1", L * FC + oc),
                              waits=[tmm, etok.get(oc - 2)], sig=True)
                    etok[oc] = bb.split(hh_[:, Q(oc)], hl_[:, Q(oc)],
                                        gsc[:, Q(oc % 2)], waits=[tg])
                th_all = bb.tok("V")
                w2tok = None
                etok2 = {}
                for oc in range(DC):
                    bb.dma("S", w2h[:, :], D["sc_w2"][L, oc, 0, :, :],
                           waits=[w2tok])
                    twl = bb.dma("S", w2l[:, :], D["sc_w2"][L, oc, 1, :, :])
                    pbank = pb[2 + (oc % 4)]
                    tmm = None
                    for fc in range(FC):
                        tmm = mm3(bb, pbank[:, :],
                                  (w2h[:, C1(fc)], w2l[:, C1(fc)]),
                                  (hh_[:, Q(fc)], hl_[:, Q(fc)]),
                                  start=(fc == 0), stop=(fc == FC - 1),
                                  waits=[twl, th_all, etok2.get(oc - 4)]
                                  if fc == 0 else [], sig=(fc == FC - 1))
                    w2tok = tmm
                    bb.v("tensor_scalar", esc[:, :], pbank[:, :],
                         P1("fb2", L * DC + oc), None, ALU.add, waits=[tmm])
                    etok2[oc] = bb.v("tensor_tensor", yt[:, Q(oc)], esc[:, :],
                                     yt[:, Q(oc)], ALU.add, sig=True)
                    bb.dma("G", D["sc_x"][qt, :, Q(oc)], yt[:, Q(oc)],
                           waits=[etok2[oc]])
                prev = bb.snapc()

        bb.run_phase(emit_ffn)


def tail(nc, bb, D, G):
    pb = G["pb"]
    mu, rstd = G["mu"], G["rstd"]
    ln_stats = G["ln_stats"]
    P1 = G["P1"]

    with contextlib.ExitStack() as S4:
        sb = lambda n, s, d=F32: S4.enter_context(nc.sbuf_tensor(n, s, d))
        xt = sb("xtt", [128, DC * QS])
        xta = sb("xta", [128, DC * QS])
        osc = sb("osc", [128, QS])
        def emit_topln(bb):
            ent = [bb.tok("T"), bb.tok("V"), bb.tok("A")]
            bb.wait("scalar", *ent)
            bb.wait("scalar", bb.tok("dG"))
            bb.wait("gpsimd", *ent)
            prev = []
            for qt in range(QT):
                t1 = bb.dma("A", xt[:, :], D["sc_x"][qt, :, :],
                            waits=prev + (ent if qt == 0 else []))
                t2 = bb.dma("A", xta[:, :], D["x0"][qt, :, :])
                tadd = None
                for dc in range(DC):
                    tadd = bb.v("tensor_tensor", xt[:, Q(dc)], xt[:, Q(dc)],
                                xta[:, Q(dc)], ALU.add,
                                waits=[t1, t2] if dc == 0 else [], sig=True)
                trs = ln_stats(xt, waits=[tadd])
                td = None
                for dc in range(DC):
                    bb.v("tensor_tensor", osc[:, :], xt[:, Q(dc)], mu[:, :],
                         ALU.subtract,
                         waits=[trs, td] if dc == 0 else [td])
                    tn = bb.v("tensor_tensor", osc[:, :], osc[:, :], rstd[:, :],
                              ALU.mult, sig=True)
                    td = bb.dma("G", D["sc_x1"][qt, :, Q(dc)], osc[:, :],
                                waits=[tn])
                prev = bb.snapc()

        bb.run_phase(emit_topln)

    with contextlib.ExitStack() as S5:
        sb = lambda n, s, d=F32: S5.enter_context(nc.sbuf_tensor(n, s, d))
        x1t = sb("x1t", [128, DC * QS])
        thh = sb("thh", [128, FC * QS], BF16)
        thl = sb("thl", [128, FC * QS], BF16)
        t1h = sb("t1h", [128, FC * QS], BF16)
        t1l = sb("t1l", [128, FC * QS], BF16)
        x1h = t1h
        x1l = t1l
        wa = sb("wa", [128, 2 * DC * 128], BF16)
        wb = sb("wb", [128, 2 * DC * 128], BF16)
        wbig = sb("wbig", [128, 2 * FC * 128], BF16)
        sA = sb("sA", [128, QS])
        sB = sb("sB", [128, QS])
        def emit_tern(bb):
            ent = [bb.tok("T"), bb.tok("V"), bb.tok("A")]
            bb.wait("scalar", *ent)
            bb.wait("scalar", bb.tok("dG"))
            bb.wait("gpsimd", *ent)
            bb.wait("sync", *ent)
            prev = []
            wtok_ab = [None, None]
            wtok_big = [None, None]
            for qt in range(QT):
                tx = bb.dma("A", x1t[:, :], D["sc_x1"][qt, :, :],
                            waits=prev + (ent if qt == 0 else []))
                for dc in range(DC):
                    bb.split(x1h[:, Q(dc)], x1l[:, Q(dc)], x1t[:, Q(dc)],
                             waits=[tx])
                tx1s = bb.tok("V")
                etok = {}
                for oc in range(FC):
                    p = oc % 2
                    wsl = slice(p * DC * 128, (p + 1) * DC * 128)
                    bb.dma("G", wa[:, wsl], D["tw1"][oc, :, :],
                           waits=[wtok_ab[p]])
                    twl = bb.dma("G", wb[:, wsl], D["tw2"][oc, :, :])
                    pa = pb[2 + (oc % 2) * 2]
                    pk = pb[3 + (oc % 2) * 2]
                    for dc in range(DC):
                        ws1 = slice(p * DC * 128 + dc * 128,
                                    p * DC * 128 + (dc + 1) * 128)
                        bb.mm(pa[:, :], wa[:, ws1], x1h[:, Q(dc)],
                              start=(dc == 0), stop=False,
                              waits=[twl, tx1s, etok.get(oc - 2)]
                              if dc == 0 else [])
                        bb.mm(pa[:, :], wa[:, ws1], x1l[:, Q(dc)],
                              start=False, stop=(dc == DC - 1))
                    tmm = None
                    for dc in range(DC):
                        ws1 = slice(p * DC * 128 + dc * 128,
                                    p * DC * 128 + (dc + 1) * 128)
                        bb.mm(pk[:, :], wb[:, ws1], x1h[:, Q(dc)],
                              start=(dc == 0), stop=False)
                        tmm = bb.mm(pk[:, :], wb[:, ws1], x1l[:, Q(dc)],
                                    start=False, stop=(dc == DC - 1),
                                    sig=(dc == DC - 1))
                    wtok_ab[p] = tmm
                    ta = bb.a("activation", sA[:, :], pa[:, :], AF.Silu,
                              scale=P1("w1s", oc),
                              waits=[tmm, etok.get(oc - 1)], sig=True)
                    bb.v("tensor_scalar", sB[:, :], pk[:, :], P1("w2s", oc),
                         None, ALU.mult, waits=[tmm])
                    thm = bb.v("tensor_tensor", sB[:, :], sA[:, :],
                               sB[:, :], ALU.mult, waits=[ta], sig=True)
                    etok[oc] = bb.split(thh[:, Q(oc)], thl[:, Q(oc)],
                                        sB[:, :], waits=[thm])
                tT1 = bb.tok("V")
                etok = {}
                for oc in range(FC):
                    p = oc % 2
                    wsl = slice(p * FC * 128, (p + 1) * FC * 128)
                    twl = bb.dma("G", wbig[:, wsl], D["th1"][oc, :, :],
                                 waits=[wtok_big[p]])
                    pbank = pb[2 + (oc % 4)]
                    tmm = None
                    for ic in range(FC):
                        ws1 = slice(p * FC * 128 + ic * 128,
                                    p * FC * 128 + (ic + 1) * 128)
                        bb.mm(pbank[:, :], wbig[:, ws1], thh[:, Q(ic)],
                              start=(ic == 0), stop=False,
                              waits=[twl, tT1, etok.get(oc - 4)]
                              if ic == 0 else [])
                        tmm = bb.mm(pbank[:, :], wbig[:, ws1], thl[:, Q(ic)],
                                    start=False, stop=(ic == FC - 1),
                                    sig=(ic == FC - 1))
                    wtok_big[p] = tmm
                    tg = bb.a("activation", sA[:, :], pbank[:, :], AF.Gelu,
                              scale=P1("h1s", oc),
                              waits=[tmm, etok.get(oc - 1)], sig=True)
                    etok[oc] = bb.split(t1h[:, Q(oc)], t1l[:, Q(oc)], sA[:, :],
                                        waits=[tg])
                tT2 = bb.tok("V")
                etok = {}
                for oc in range(FC):
                    p = oc % 2
                    wsl = slice(p * FC * 128, (p + 1) * FC * 128)
                    twl = bb.dma("G", wbig[:, wsl], D["th2"][oc, :, :],
                                 waits=[wtok_big[p]])
                    pbank = pb[2 + (oc % 4)]
                    tmm = None
                    for ic in range(FC):
                        ws1 = slice(p * FC * 128 + ic * 128,
                                    p * FC * 128 + (ic + 1) * 128)
                        bb.mm(pbank[:, :], wbig[:, ws1], t1h[:, Q(ic)],
                              start=(ic == 0), stop=False,
                              waits=[twl, tT2, etok.get(oc - 4)]
                              if ic == 0 else [])
                        tmm = bb.mm(pbank[:, :], wbig[:, ws1], t1l[:, Q(ic)],
                                    start=False, stop=(ic == FC - 1),
                                    sig=(ic == FC - 1))
                    wtok_big[p] = tmm
                    tg = bb.a("activation", sA[:, :], pbank[:, :], AF.Gelu,
                              scale=P1("h2s", oc),
                              waits=[tmm, etok.get(oc - 1)], sig=True)
                    etok[oc] = bb.split(thh[:, Q(oc)], thl[:, Q(oc)], sA[:, :],
                                        waits=[tg])
                tT3 = bb.tok("V")
                tprev = None
                for oc in range(DC):
                    p = oc % 2
                    wsl = slice(p * FC * 128, (p + 1) * FC * 128)
                    twl = bb.dma("G", wbig[:, wsl], D["tw3"][oc, :, :],
                                 waits=[wtok_big[p]])
                    tmm = None
                    for ic in range(FC):
                        ws1 = slice(p * FC * 128 + ic * 128,
                                    p * FC * 128 + (ic + 1) * 128)
                        bb.mm(pb[6][:, :], wbig[:, ws1], thh[:, Q(ic)],
                              start=(ic == 0), stop=False,
                              waits=[twl, tT3, tprev] if ic == 0 else [])
                        tmm = bb.mm(pb[6][:, :], wbig[:, ws1], thl[:, Q(ic)],
                                    start=False, stop=(ic == FC - 1),
                                    sig=(ic == FC - 1))
                    wtok_big[p] = tmm
                    bb.v("tensor_scalar", sA[:, :], pb[6][:, :], P1("w3s", oc),
                         None, ALU.mult, waits=[tmm])
                    tprev = bb.v("tensor_tensor", x1t[:, Q(oc)], sA[:, :],
                                 x1t[:, Q(oc)], ALU.add, sig=True)
                trs = ln_stats(x1t, waits=[tprev])
                td = None
                for dc in range(DC):
                    bb.v("tensor_tensor", sA[:, :], x1t[:, Q(dc)], mu[:, :],
                         ALU.subtract,
                         waits=[trs, td] if dc == 0 else [td])
                    tn = bb.v("tensor_tensor", sA[:, :], sA[:, :], rstd[:, :],
                              ALU.mult, sig=True)
                    td = bb.dma("S", D["out"][qt, :, Q(dc)], sA[:, :],
                                waits=[tn])
                prev = bb.snapc()

        bb.run_phase(emit_tern)


_GAMMAS = 1.0 - np.exp(np.linspace(np.log(1 / 32), np.log(1 / 512), H))


def host_tables(parity):
    g = np.arange(parity * T, parity * T + T, dtype=np.float64)
    base = (np.arange(0, 128, 2, dtype=np.float64) + 0.4 * 128) / (1.4 * 128)
    sc = base[None, :] ** (g / 512.0)[:, None]
    inv = 1.0 / 10000.0 ** (np.arange(64, dtype=np.float64) / 64)
    ang = g[:, None] * inv[None, :]
    sin, cos = np.sin(ang), np.cos(ang)
    loc = (np.arange(T) % CH).astype(np.float64)
    dupi = np.arange(128) // 2
    sgn = np.where(np.arange(128) % 2 == 0, -1.0, 1.0)
    tabs = np.zeros((H, 4, 128, T), dtype=np.float64)
    for h in range(H):
        gq = _GAMMAS[h] ** loc
        gk = _GAMMAS[h] ** (-loc)
        tabs[h, 0] = (cos * sc)[:, dupi].T * gq[None, :]
        tabs[h, 1] = (sin * sc)[:, dupi].T * gq[None, :] * sgn[:, None]
        tabs[h, 2] = (cos / sc)[:, dupi].T * gk[None, :]
        tabs[h, 3] = (sin / sc)[:, dupi].T * gk[None, :] * sgn[:, None]
    t4 = tabs.reshape(H, 4, 128, QT, QS).transpose(0, 3, 2, 1, 4)
    return np.ascontiguousarray(
        t4.reshape(H, QT, 128, 4 * QS).astype(np.float32))


def host_consts():
    cons = np.zeros((128, NCON), dtype=np.float32)
    m = (np.arange(128)[None, :] >= np.arange(128)[:, None])
    cons[:, CON_MASK:CON_MASK + 128] = m.astype(np.float32)
    cons[:, CON_O128:CON_O128 + 128] = 1.0 / 128
    cons[:, CON_O1024:CON_O1024 + 128] = 1.0 / 1024
    for h in range(H):
        cons[:, CON_GAM + h * 128:CON_GAM + (h + 1) * 128] = np.float32(
            _GAMMAS[h] ** 128)
    cons[:, CON_EPS] = EPS
    cons[:, CON_NH] = -0.5
    return cons


def host_pars(I):
    p = np.zeros((128, NPAR), dtype=np.float32)

    def put(nm, arr):
        a = np.asarray(arr)
        p[:, _PAR[nm]:_PAR[nm] + a.shape[1]] = a

    put("ln1w", I["L0_attn_ln1_w"].reshape(2, 8, 128).transpose(2, 0, 1)
        .reshape(128, 16))
    put("ln1b", I["L0_attn_ln1_b"].reshape(2, 8, 128).transpose(2, 0, 1)
        .reshape(128, 16))
    put("ln2w", I["L0_attn_ln2_w"].reshape(2, 8, 128).transpose(2, 0, 1)
        .reshape(128, 16))
    put("ln2b", I["L0_attn_ln2_b"].reshape(2, 8, 128).transpose(2, 0, 1)
        .reshape(128, 16))
    put("gnw", I["L0_attn_gn_w"].reshape(2, 8, 128).transpose(2, 0, 1)
        .reshape(128, 16))
    put("gnb", I["L0_attn_gn_b"].reshape(2, 8, 128).transpose(2, 0, 1)
        .reshape(128, 16))
    put("fb1", I["L0_attn_ffn_b1"].reshape(2, 32, 128).transpose(2, 0, 1)
        .reshape(128, 64))
    put("fb2", I["L0_attn_ffn_b2"].reshape(2, 8, 128).transpose(2, 0, 1)
        .reshape(128, 16))
    put("w1s", I["L0_ffn_w1_s"].reshape(32, 128).T)
    put("w2s", I["L0_ffn_w2_s"].reshape(32, 128).T)
    put("h1s", I["L0_ffn_h1_s"].reshape(32, 128).T)
    put("h2s", I["L0_ffn_h2_s"].reshape(32, 128).T)
    put("w3s", I["L0_ffn_w3_s"].reshape(8, 128).T)
    return p


def host_weights(I):
    w = {}

    def c(a):
        return np.ascontiguousarray(a, dtype=np.float32)

    for nm, key in (("wq", "L0_attn_Wq"), ("wk", "L0_attn_Wk"),
                    ("wv", "L0_attn_Wv")):
        a = I[key].reshape(2, H, DC, 128, 128).transpose(0, 1, 3, 2, 4)
        w[nm] = c(a.reshape(2, H, 128, DC * 128))
    a = I["L0_attn_W_G"].reshape(2, DC, 128, H, 128).transpose(0, 3, 2, 1, 4)
    w["wg"] = c(a.reshape(2, H, 128, DC * 128))
    a = I["L0_attn_W_O"].reshape(2, DC, 128, DC, 128).transpose(0, 3, 2, 1, 4)
    w["wo"] = c(a.reshape(2, DC, 128, DC * 128))
    a = I["L0_attn_ffn_w1"].reshape(2, DC, 128, FC, 128).transpose(
        0, 3, 2, 1, 4)
    w["fw1"] = c(a.reshape(2, FC, 128, DC * 128))
    a = I["L0_attn_ffn_w2"].reshape(2, FC, 128, DC, 128).transpose(
        0, 3, 2, 1, 4)
    w["fw2"] = c(a.reshape(2, DC, 128, FC * 128))
    for nm, key, n_in, n_out in (("tw1", "L0_ffn_w1_w", DC, FC),
                                 ("tw2", "L0_ffn_w2_w", DC, FC),
                                 ("th1", "L0_ffn_h1_w", FC, FC),
                                 ("th2", "L0_ffn_h2_w", FC, FC),
                                 ("tw3", "L0_ffn_w3_w", FC, DC)):
        a = np.ascontiguousarray(I[key].T)
        a = a.reshape(n_in, 128, n_out, 128).transpose(2, 1, 0, 3)
        w[nm] = c(a.reshape(n_out, 128, n_in * 128))
    return w


_CACHE = {}


def kernel(**inputs):
    I = {k: np.asarray(v) for k, v in inputs.items()}
    if "nc" not in _CACHE:
        _CACHE["nc"] = build()
    nc = _CACHE["nc"]

    w = host_weights(I)
    pars = host_pars(I)
    cons = host_consts()
    consb = np.eye(128, dtype=BF)
    tabs = [host_tables(0), host_tables(1)]
    X = I["L0_attn_X"]

    in_maps = []
    for c in range(8):
        s, par = c // 2, c % 2
        xs = np.ascontiguousarray(X[s, par * T:(par + 1) * T, :].T,
                                  dtype=np.float32)
        x0 = xs.reshape(DC, 128, QT, QS).transpose(2, 1, 0, 3)
        x0 = np.ascontiguousarray(x0.reshape(QT, 128, DC * QS))
        m = dict(w)
        m.update(x0=x0, xtab=tabs[par], pars=pars, cons=cons, consb=consb,
                 maska=np.full((128, 1), float(par), np.float32))
        in_maps.append(m)

    res = run_bass_kernel_spmd(nc, in_maps, core_ids=list(range(8)))
    out = np.empty((4, 4096, 1024), dtype=np.float32)
    for c in range(8):
        s, par = c // 2, c % 2
        o = np.asarray(res.results[c]["out"])
        o = o.reshape(QT, 128, DC, QS).transpose(2, 1, 0, 3)
        out[s, par * T:(par + 1) * T, :] = o.reshape(1024, T).T
    return out


# version 11
# speedup vs baseline: 1.1088x; 1.0560x over previous
import contextlib
import numpy as np
import ml_dtypes
import concourse.bass as bass
import concourse.mybir as mybir
from concourse.bass_utils import run_bass_kernel_spmd

F32 = mybir.dt.float32
BF16 = mybir.dt.bfloat16
AF = mybir.ActivationFunctionType
ALU = mybir.AluOpType
BF = ml_dtypes.bfloat16

T = 2048
QT = 4
QS = 512
CH = 128
DC = 8
FC = 32
H = 8
EPS = 1e-5
SHUF = [i ^ 1 for i in range(32)]

_PAR = {}
_c = 0
for _nm, _n in [("ln1w", 16), ("ln1b", 16), ("ln2w", 16), ("ln2b", 16),
                ("gnw", 16), ("gnb", 16), ("fb1", 64), ("fb2", 16),
                ("w1s", 32), ("w2s", 32), ("h1s", 32), ("h2s", 32),
                ("w3s", 8)]:
    _PAR[_nm] = _c
    _c += _n
NPAR = _c

CON_MASK = 0
CON_O128 = 128
CON_O1024 = 256
CON_GAM = 384
CON_EPS = 384 + H * 128
CON_NH = CON_EPS + 1
NCON = CON_EPS + 2


_B_DRAINS = [False]


def _toks(waits):
    return [w for w in waits if w is not None]


class B:

    ENGS = ("tensor", "vector", "scalar", "sync", "gpsimd")
    DQ = {"dS": "sync", "dG": "gpsimd", "dA": "scalar"}
    OWNER = {"sync": "dS", "gpsimd": "dG", "scalar": "dA"}

    def __init__(self, nc, sems):
        self.nc, self.sems = nc, sems
        self.val = {k: 0 for k in sems}
        self.hw = {}
        self.mode = None
        self.e = None
        self.scout = False
        self.groups = {s: [0] for s in ("dS", "dG", "dA")}

    def _flush(self):
        for sem, eng in self.DQ.items():
            self.wait(eng, self.tok(sem))

    def run_phase(self, emit):
        ev, eh = dict(self.val), dict(self.hw)
        self.scout, self.mode, self.e = True, None, None
        self.val, self.hw = dict(ev), dict(eh)
        emit(self)
        self._flush()
        self.scout = False
        ret = [None]
        with self.nc.Block() as blk:
            for nm in self.ENGS:
                def body(e, nm=nm):
                    self.val, self.hw = dict(ev), dict(eh)
                    self.mode, self.e = nm, e
                    ret[0] = emit(self)
                    self._flush()
                getattr(blk, nm)(body)
        self.mode, self.e = None, None
        return ret[0]

    def _inc(self, ins, sem, amt):
        if ins is not None:
            ins.then_inc(self.sems[sem], amt)
        self.val[sem] += amt
        return (sem, self.val[sem])

    def tok(self, sem):
        return (sem, self.val[sem])

    def _fix(self, sem, v):
        g = self.groups[sem]
        import bisect
        i = bisect.bisect_left(g, v)
        if i == len(g):
            raise RuntimeError(f"no group end >= {v} for {sem} (have {g[-5:]})")
        return g[i]

    def wait(self, engname, *tokens):
        for t in tokens:
            if t is None:
                continue
            sem, v0 = t
            if v0 <= 0:
                continue
            key = (engname, sem)
            if self.hw.get(key, 0) >= v0:
                continue
            own = self.OWNER.get(engname)
            if self.scout:
                if own is not None:
                    g = self.groups[own]
                    if g[-1] < self.val[own]:
                        g.append(self.val[own])
            else:
                if own is not None:
                    skey = (engname, "_self")
                    if self.val[own] > self.hw.get(skey, 0):
                        if self.mode == engname:
                            self.e.wait_ge(self.sems[own], self.val[own])
                        self.hw[skey] = self.val[own]
                if self.mode == engname:
                    v = self._fix(sem, v0) if sem in self.groups else v0
                    self.e.wait_ge(self.sems[sem], v)
            self.hw[key] = v0

    def snapc(self):
        return [self.tok(s) for s in ("T", "V", "A", "dA", "dG", "dS")]

    def dma(self, q, out, in_, waits=()):
        ename = {"S": "sync", "G": "gpsimd", "A": "scalar"}[q]
        sem = {"S": "dS", "G": "dG", "A": "dA"}[q]
        self.wait(ename, *_toks(waits))
        ins = (self.e.dma_start(out=out, in_=in_)
               if self.mode == ename else None)
        return self._inc(ins, sem, 16)

    def mm(self, out, lhsT, rhs, start, stop, waits=(), sig=False):
        self.wait("tensor", *_toks(waits))
        ins = (self.e.matmul(out, lhsT, rhs, start=start, stop=stop)
               if self.mode == "tensor" else None)
        return self._inc(ins, "T", 1) if sig else None

    def tr(self, out, in_, ident, waits=(), sig=False):
        self.wait("tensor", *_toks(waits))
        ins = (self.e.transpose(out, in_, ident)
               if self.mode == "tensor" else None)
        return self._inc(ins, "T", 1) if sig else None

    def v(self, name, *args, waits=(), sig=False, **kw):
        self.wait("vector", *_toks(waits))
        ins = None
        if self.mode == "vector":
            if _B_DRAINS[0]:
                self.e.drain()
            ins = getattr(self.e, name)(*args, **kw)
        return self._inc(ins, "V", 1) if sig else None

    def a(self, name, *args, waits=(), sig=False, **kw):
        self.wait("scalar", *_toks(waits))
        ins = None
        if self.mode == "scalar":
            if _B_DRAINS[0]:
                self.e.drain()
            ins = getattr(self.e, name)(*args, **kw)
        return self._inc(ins, "A", 1) if sig else None

    def collective(self, kind, op, replica_groups, ins_, outs_, waits=()):
        self.wait("gpsimd", *_toks(waits))
        ins = (self.e.collective_compute(kind, op,
                                         replica_groups=replica_groups,
                                         ins=ins_, outs=outs_)
               if self.mode == "gpsimd" else None)
        return self._inc(ins, "C", 1)

    def split(self, hi, lo, src, waits=(), sig=True):
        self.v("tensor_copy", hi, src, waits=waits)
        return self.v("tensor_tensor", lo, src, hi, ALU.subtract, sig=sig)


def mm3(bb, out, wpair, apair, start, stop, waits=(), sig=False):
    whi, wlo = wpair
    ahi, alo = apair
    bb.mm(out, whi, ahi, start=start, stop=False, waits=waits)
    bb.mm(out, whi, alo, start=False, stop=False)
    return bb.mm(out, wlo, ahi, start=False, stop=stop, sig=sig)


def Q(i):
    return slice(i * QS, (i + 1) * QS)


def C1(i):
    return slice(i * 128, (i + 1) * 128)


def build(debug=False, stop=None, racecheck=False):
    nc = bass.Bass("TRN2", target_bir_lowering=False,
                   detect_race_conditions=racecheck)
    _B_DRAINS[0] = racecheck

    def din(n, s, d=F32):
        return nc.dram_tensor(n, s, d, kind="ExternalInput")

    def dscr(n, s, d=F32):
        if debug:
            return nc.dram_tensor(n, s, d, kind="ExternalOutput")
        return nc.dram_tensor(n, s, d)

    D = {}
    D["x0"] = din("x0", [QT, 128, DC * QS])
    D["wq"] = din("wq", [2, H, 128, DC * 128])
    D["wk"] = din("wk", [2, H, 128, DC * 128])
    D["wv"] = din("wv", [2, H, 128, DC * 128])
    D["wg"] = din("wg", [2, H, 128, DC * 128])
    D["wo"] = din("wo", [2, DC, 128, DC * 128])
    D["fw1"] = din("fw1", [2, FC, 128, DC * 128])
    D["fw2"] = din("fw2", [2, DC, 128, FC * 128])
    D["tw1"] = din("tw1", [FC, 128, DC * 128])
    D["tw2"] = din("tw2", [FC, 128, DC * 128])
    D["th1"] = din("th1", [FC, 128, FC * 128])
    D["th2"] = din("th2", [FC, 128, FC * 128])
    D["tw3"] = din("tw3", [DC, 128, FC * 128])
    D["xtab"] = din("xtab", [H, QT, 128, 4 * QS])
    D["pars"] = din("pars", [128, NPAR])
    D["cons"] = din("cons", [128, NCON])
    D["consb"] = din("consb", [128, 128], BF16)
    D["maska"] = din("maska", [128, 1])
    D["out"] = nc.dram_tensor("out", [QT, 128, DC * QS], F32,
                              kind="ExternalOutput")
    D["sc_y"] = dscr("sc_y", [QT, 128, DC * QS])
    D["sc_x"] = dscr("sc_x", [QT, 128, DC * QS])
    D["sc_x1"] = dscr("sc_x1", [QT, 128, DC * QS])
    D["sc_ln1"] = dscr("sc_ln1", [QT, 2, 128, DC * QS], BF16)
    D["sc_kv"] = dscr("sc_kv", [H, QT, 6, 128, QS], BF16)
    D["sc_w1"] = dscr("sc_w1", [2, FC, 2, 128, DC * 128], BF16)
    D["sc_w2"] = dscr("sc_w2", [2, DC, 2, 128, FC * 128], BF16)
    D["cc_in"] = dscr("cc_in", [128, H * 128])
    D["cc_out"] = dscr("cc_out", [256, H * 128])

    with contextlib.ExitStack() as ST:
        sems = {n: ST.enter_context(nc.semaphore(n))
                for n in ("dS", "dG", "dA", "T", "V", "A", "C")}
        bb = B(nc, sems)

        sbp = lambda n, s, d=F32: ST.enter_context(nc.sbuf_tensor(n, s, d))
        G = {}
        G["c_par"] = sbp("c_par", [128, NPAR])
        G["c_con"] = sbp("c_con", [128, NCON])
        G["c_idb"] = sbp("c_idb", [128, 128], BF16)
        G["c_ma"] = sbp("c_ma", [128, 1])
        G["sq"] = sbp("sq", [128, 2 * QS])
        G["mu"] = sbp("mu", [128, QS])
        G["msq"] = sbp("msq", [128, QS])
        G["var"] = sbp("var", [128, QS])
        G["rstd"] = sbp("rstd", [128, QS])
        G["lnf"] = sbp("lnf", [128, QS])
        G["pb"] = [ST.enter_context(nc.psum_tensor(f"pb{i}", [128, 512], F32))
                   for i in range(7)]
        G["pt"] = ST.enter_context(nc.psum_tensor("pt", [128, 1024], BF16))

        cc = G["c_con"]
        G["MASK"] = cc[:, CON_MASK:CON_MASK + 128]
        G["O128"] = cc[:, CON_O128:CON_O128 + 128]
        G["O1024"] = cc[:, CON_O1024:CON_O1024 + 128]
        G["GAM"] = lambda h: cc[:, CON_GAM + h * 128:CON_GAM + (h + 1) * 128]
        G["P1"] = lambda nm, i: G["c_par"][:, _PAR[nm] + i:_PAR[nm] + i + 1]
        G["IDB"] = G["c_idb"][:, :]
        G["EPSA"] = cc[:, CON_EPS:CON_EPS + 1]
        G["NHA"] = cc[:, CON_NH:CON_NH + 1]

        def emit_boot(bb):
            bb.dma("S", G["c_par"][:, :], D["pars"][:, :])
            bb.dma("S", G["c_con"][:, :], D["cons"][:, :])
            bb.dma("S", G["c_idb"][:, :], D["consb"][:, :])
            return bb.dma("S", G["c_ma"][:, :], D["maska"][:, :])

        G["boot"] = bb.run_phase(emit_boot)

        def ln_stats(src, waits=()):
            pb = G["pb"]
            sq, mu, msq, var, rstd = (G[k] for k in
                                      ("sq", "mu", "msq", "var", "rstd"))
            tsq = [None, None]
            for dc in range(DC):
                xs = src[:, Q(dc)]
                w0 = list(waits) if dc == 0 else []
                tA = bb.a("activation", sq[:, Q(dc % 2)], xs, AF.Square,
                          waits=w0 + [tsq[dc % 2]], sig=True)
                bb.mm(pb[0][:, :], G["O1024"], xs, start=(dc == 0),
                      stop=(dc == DC - 1), waits=w0)
                tsq[dc % 2] = bb.mm(pb[1][:, :], G["O1024"], sq[:, Q(dc % 2)],
                                    start=(dc == 0), stop=(dc == DC - 1),
                                    waits=[tA], sig=True)
            tst = bb.tok("T")
            bb.v("tensor_copy", mu[:, :], pb[0][:, :], waits=[tst])
            bb.v("tensor_tensor", msq[:, :], mu[:, :], mu[:, :], ALU.mult)
            tv = bb.v("tensor_tensor", var[:, :], pb[1][:, :], msq[:, :],
                      ALU.subtract, sig=True)
            bb.a("activation", rstd[:, :], var[:, :], AF.Ln,
                 bias=G["EPSA"], waits=[tv])
            return bb.a("activation", rstd[:, :], rstd[:, :], AF.Exp,
                        scale=G["NHA"], sig=True)

        G["ln_stats"] = ln_stats
        lay_src = [D["x0"], D["sc_x"]]

        done = False
        with contextlib.ExitStack() as SLY:
            sly = lambda n, s, d=F32: SLY.enter_context(
                nc.sbuf_tensor(n, s, d))
            G["s_slab"] = sly("s_slab", [128, H * 128])
            G["s_hi"] = sly("s_hi", [128, 128], BF16)
            G["s_lo"] = sly("s_lo", [128, 128], BF16)
            G["srecv"] = sly("srecv", [128, H * 128])
            G["bias_wo"] = sly("bias_wo", [128, DC])
            for L in range(2):
                attn_layer(nc, bb, D, G, L, lay_src, stop)
                if stop in (f"L{L}", f"R1L{L}", f"R2L{L}"):
                    done = True
                    break
        if not done:
            tail(nc, bb, D, G)

        def emit_fin(bb):
            bb.wait("sync", bb.tok("dS"))
            bb.wait("gpsimd", bb.tok("dG"))
            bb.wait("scalar", bb.tok("dA"))

        bb.run_phase(emit_fin)
    return nc


def attn_layer(nc, bb, D, G, L, lay_src, stop=None):
    pb, pt = G["pb"], G["pt"]
    P1, MASK, O128, GAM, IDB = (G[k] for k in
                                ("P1", "MASK", "O128", "GAM", "IDB"))
    s_slab, s_hi, s_lo, srecv = (G[k] for k in
                                 ("s_slab", "s_hi", "s_lo", "srecv"))
    mu, msq, lnf, rstd = (G[k] for k in ("mu", "msq", "lnf", "rstd"))
    ln_stats = G["ln_stats"]
    boot = G["boot"]

    with contextlib.ExitStack() as S1:
        sb = lambda n, s, d=F32: S1.enter_context(
            nc.sbuf_tensor(n + str(L), s, d))
        wkh = sb("wkh", [128, H * DC * 128], BF16)
        wkl = sb("wkl", [128, H * DC * 128], BF16)
        wvh = sb("wvh", [128, H * DC * 128], BF16)
        wvl = sb("wvl", [128, H * DC * 128], BF16)
        wland = sb("wland", [128, 2 * DC * 128])
        xt = sb("xt", [128, DC * QS])
        l1h = sb("l1h", [128, DC * QS], BF16)
        l1l = sb("l1l", [128, DC * QS], BF16)
        ttab = sb("ttab", [128, 2 * 2 * QS])
        kfh = sb("kfh", [128, 2 * QS], BF16)
        kfl = sb("kfl", [128, 2 * QS], BF16)
        vfh = sb("vfh", [128, 2 * QS], BF16)
        vfl = sb("vfl", [128, 2 * QS], BF16)
        kth = sb("kth", [128, 2 * QS], BF16)
        ktl = sb("ktl", [128, 2 * QS], BF16)
        vth = sb("vth", [128, 2 * QS], BF16)
        vtl = sb("vtl", [128, 2 * QS], BF16)
        scA = sb("scA", [128, 2 * QS])
        scB = sb("scB", [128, 2 * QS])
        f1l = sb("f1l", [128, DC * 128])
        f1h = sb("f1h", [128, DC * 128], BF16)
        f1lo = sb("f1lo", [128, DC * 128], BF16)
        f2l = sb("f2l", [128, FC * 64])
        f2h = sb("f2h", [128, FC * 64], BF16)
        f2lo = sb("f2lo", [128, FC * 64], BF16)

        def emit_r1(bb):
            ent = [bb.tok("T"), bb.tok("V"), bb.tok("A")]
            bb.wait("sync", *ent)
            bb.wait("scalar", *ent)
            bb.wait("scalar", bb.tok("dG"))
            bb.wait("gpsimd", *ent)

            wtok = [None, None]
            for i, (key, dh, dl) in enumerate([("wk", wkh, wkl),
                                               ("wv", wvh, wvl)]):
                for h in range(H):
                    p = (i * H + h) % 2
                    sl = slice(p * DC * 128, (p + 1) * DC * 128)
                    dsl = slice(h * DC * 128, (h + 1) * DC * 128)
                    tl = bb.dma("S", wland[:, sl], D[key][L, h, :, :],
                                waits=[wtok[p], boot])
                    wtok[p] = bb.split(dh[:, dsl], dl[:, dsl], wland[:, sl],
                                       waits=[tl])
            w_r1 = bb.tok("V")

            jobs = []
            jst = {"f1": [None], "f2": [None]}

            def mk_job(kind, oc, half=0):
                def go():
                    if kind == "f1":
                        land, hi, lo = f1l, f1h, f1lo
                        src = D["fw1"][L, oc, :, :]
                        dsth = D["sc_w1"][L, oc, 0, :, :]
                        dstl = D["sc_w1"][L, oc, 1, :, :]
                    else:
                        n = FC * 64
                        land, hi, lo = f2l, f2h, f2lo
                        src = D["fw2"][L, oc, :, half * n:(half + 1) * n]
                        dsth = D["sc_w2"][L, oc, 0, :, half * n:(half + 1) * n]
                        dstl = D["sc_w2"][L, oc, 1, :, half * n:(half + 1) * n]
                    tl = bb.dma("S", land[:, :], src, waits=[jst[kind][0]])
                    ts = bb.split(hi[:, :], lo[:, :], land[:, :], waits=[tl])
                    bb.dma("G", dsth, hi[:, :], waits=[ts])
                    jst[kind][0] = bb.dma("G", dstl, lo[:, :])
                return go

            for oc in range(FC):
                jobs.append(mk_job("f1", oc))
            for oc in range(DC):
                jobs.append(mk_job("f2", oc, 0))
                jobs.append(mk_job("f2", oc, 1))

            bb.v("memset", s_slab[:, :], 0.0)

            itok = {}
            gh = 0
            prev_tks = prev_tvs = prev_tve = prev_tSm = None
            for qt in range(QT):
                txw = itok.get(gh - 1, []) + [boot] + (ent if gh == 0 else [])
                tx = bb.dma("A", xt[:, :], lay_src[L][qt, :, :], waits=txw)
                trs = ln_stats(xt, waits=[tx])
                for dc in range(DC):
                    bb.v("tensor_tensor", scA[:, Q(0)], xt[:, Q(dc)], mu[:, :],
                         ALU.subtract, waits=[trs] if dc == 0 else [])
                    bb.v("tensor_tensor", scA[:, Q(0)], scA[:, Q(0)], rstd[:, :],
                         ALU.mult)
                    tln = bb.v("tensor_scalar", lnf[:, :], scA[:, Q(0)],
                               P1("ln1w", L * DC + dc),
                               P1("ln1b", L * DC + dc),
                               ALU.mult, ALU.add, sig=True)
                    bb.split(l1h[:, Q(dc)], l1l[:, Q(dc)], lnf[:, :],
                             waits=[tln])
                tl1 = bb.tok("V")
                bb.dma("G", D["sc_ln1"][qt, 0, :, :], l1h[:, :], waits=[tl1])
                bb.dma("G", D["sc_ln1"][qt, 1, :, :], l1l[:, :])

                for h in range(H):
                    p = h % 2
                    pw = itok.get(gh - 2, [])
                    bb.wait("tensor", *pw)
                    bb.wait("vector", *pw)
                    if jobs:
                        jobs.pop(0)()
                    psl = slice(p * 2 * QS, (p + 1) * 2 * QS)
                    ttb = bb.dma("A", ttab[:, psl],
                                 D["xtab"][h, qt, :, 2 * QS:4 * QS], waits=pw)
                    TAB = lambda i: ttab[:, p * 2 * QS + (i - 2) * QS:
                                         p * 2 * QS + (i - 1) * QS]
                    PP = lambda buf: buf[:, Q(p)]
                    KS = lambda buf, cci: buf[:, p * QS + cci * 128:
                                              p * QS + (cci + 1) * 128]

                    tK = None
                    for dc in range(DC):
                        hh = slice(h * DC * 128 + dc * 128,
                                   h * DC * 128 + (dc + 1) * 128)
                        tK = mm3(bb, pb[2][:, :], (wkh[:, hh], wkl[:, hh]),
                                 (l1h[:, Q(dc)], l1l[:, Q(dc)]),
                                 start=(dc == 0), stop=(dc == DC - 1),
                                 waits=[w_r1, tl1, prev_tks] if dc == 0 else [],
                                 sig=(dc == DC - 1))
                    bb.v("tensor_tensor", PP(scA), pb[2][:, :], TAB(2), ALU.mult,
                         waits=[tK, ttb])
                    bb.v("tensor_copy", PP(scB), pb[2][:, :])
                    bb.v("stream_shuffle", lnf[:, :], PP(scB), SHUF)
                    bb.v("tensor_tensor", lnf[:, :], lnf[:, :], TAB(3), ALU.mult)
                    txp = bb.v("tensor_tensor", PP(scA), PP(scA), lnf[:, :],
                               ALU.add, sig=True)
                    tks = bb.split(PP(kfh), PP(kfl), PP(scA), waits=[txp])
                    tV = None
                    for dc in range(DC):
                        hh = slice(h * DC * 128 + dc * 128,
                                   h * DC * 128 + (dc + 1) * 128)
                        tV = mm3(bb, pb[3][:, :], (wvh[:, hh], wvl[:, hh]),
                                 (l1h[:, Q(dc)], l1l[:, Q(dc)]),
                                 start=(dc == 0), stop=(dc == DC - 1),
                                 waits=[prev_tvs] if dc == 0 else [],
                                 sig=(dc == DC - 1))
                    tvc = bb.v("tensor_copy", PP(scB), pb[3][:, :],
                               waits=[tV], sig=True)
                    tvs = bb.split(PP(vfh), PP(vfl), PP(scB), waits=[tvc])
                    for cci in range(4):
                        bb.tr(pt[:, C1(cci)], KS(kfh, cci), IDB,
                              waits=[tks, prev_tve] if cci == 0 else [])
                        bb.tr(pt[:, C1(4 + cci)], KS(kfl, cci), IDB,
                              sig=(cci == 3))
                    tt1 = bb.tok("T")
                    bb.v("tensor_copy", PP(kth), pt[:, 0:QS], waits=[tt1])
                    tke = bb.v("tensor_copy", PP(ktl), pt[:, QS:2 * QS],
                               sig=True)
                    for cci in range(4):
                        bb.tr(pt[:, C1(cci)], KS(vfh, cci), IDB,
                              waits=[tvs, tke] if cci == 0 else [])
                        bb.tr(pt[:, C1(4 + cci)], KS(vfl, cci), IDB,
                              sig=(cci == 3))
                    tt2 = bb.tok("T")
                    bb.v("tensor_copy", PP(vth), pt[:, 0:QS], waits=[tt2])
                    tve = bb.v("tensor_copy", PP(vtl), pt[:, QS:2 * QS],
                               sig=True)
                    for cci in range(4):
                        bb.mm(pb[5][:, 0:128], KS(kth, cci), KS(vth, cci),
                              start=True, stop=False,
                              waits=[tve, prev_tSm] if cci == 0 else [prev_tSm])
                        bb.mm(pb[5][:, 0:128], KS(ktl, cci), KS(vth, cci),
                              start=False, stop=False)
                        tS = bb.mm(pb[5][:, 0:128], KS(kth, cci), KS(vtl, cci),
                                   start=False, stop=True, sig=True)
                        bb.v("tensor_tensor", s_slab[:, C1(h)], s_slab[:, C1(h)],
                             pb[5][:, 0:128], ALU.add, waits=[tS])
                        prev_tSm = bb.v("tensor_tensor", s_slab[:, C1(h)],
                                        s_slab[:, C1(h)], GAM(h), ALU.mult,
                                        sig=True)
                    t_all = [bb.tok("V"), bb.tok("A")]
                    bb.dma("G", D["sc_kv"][h, qt, 0, :, :], PP(kfh),
                           waits=t_all)
                    bb.dma("G", D["sc_kv"][h, qt, 1, :, :], PP(kfl))
                    bb.dma("G", D["sc_kv"][h, qt, 2, :, :], PP(kth))
                    bb.dma("G", D["sc_kv"][h, qt, 3, :, :], PP(ktl))
                    bb.dma("G", D["sc_kv"][h, qt, 4, :, :], PP(vth))
                    bb.dma("G", D["sc_kv"][h, qt, 5, :, :], PP(vtl))
                    prev_tks, prev_tvs, prev_tve = tks, tvs, tve
                    itok[gh] = bb.snapc()
                    gh += 1
            while jobs:
                jobs.pop(0)()

            tSfin = bb.tok("V")
            bb.dma("G", D["cc_in"][:, :], s_slab[:, :], waits=[tSfin])
            bb.wait("gpsimd", bb.tok("dG"))
            tcc = bb.collective(
                "AllGather", ALU.bypass,
                [[0, 1], [2, 3], [4, 5], [6, 7]],
                [D["cc_in"].ap().opt()], [D["cc_out"].ap().opt()])
            trecv = bb.dma("G", srecv[:, :], D["cc_out"][0:128, :], waits=[tcc])
            return bb.v("tensor_scalar", s_slab[:, :], srecv[:, :],
                        G["c_ma"][:, 0:1], None, ALU.mult, waits=[trecv],
                        sig=True)

        tinit = bb.run_phase(emit_r1)

    if stop == f"R1L{L}":
        return
    with contextlib.ExitStack() as S2:
        sb = lambda n, s, d=F32: S2.enter_context(
            nc.sbuf_tensor(n + "r2" + str(L), s, d))
        wqh = sb("wqh", [128, H * DC * 128], BF16)
        wql = sb("wql", [128, H * DC * 128], BF16)
        wgh = sb("wgh", [128, 2 * DC * 128], BF16)
        wgl = sb("wgl", [128, 2 * DC * 128], BF16)
        wland = sb("wland", [128, 2 * DC * 128])
        l1h = sb("l1h", [128, DC * QS], BF16)
        l1l = sb("l1l", [128, DC * QS], BF16)
        ttab = sb("ttab", [128, 2 * 2 * QS])
        kfh = sb("kfh", [128, 2 * QS], BF16)
        kfl = sb("kfl", [128, 2 * QS], BF16)
        kth = sb("kth", [128, 2 * QS], BF16)
        ktl = sb("ktl", [128, 2 * QS], BF16)
        vth = sb("vth", [128, 2 * QS], BF16)
        vtl = sb("vtl", [128, 2 * QS], BF16)
        qph = sb("qph", [128, 2 * QS], BF16)
        qpl = sb("qpl", [128, 2 * QS], BF16)
        sch = sb("sch", [128, 4 * 128], BF16)
        scl = sb("scl", [128, 4 * 128], BF16)
        scA = sb("scA", [128, 2 * QS])
        scB = sb("scB", [128, 2 * QS])
        ymu = sb("ymu", [128, H * QS])
        vst = sb("vst", [128, H * QS])
        gyh = sb("gyh", [128, H * QS], BF16)
        gyl = sb("gyl", [128, H * QS], BF16)
        wos = sb("wos", [128, DC * 128])
        wohi = sb("wohi", [128, DC * 128], BF16)
        wolo = sb("wolo", [128, DC * 128], BF16)
        woev = sb("woev", [128, QS])
        xt2 = sb("xt2", [128, 2 * QS])
        woland = wland

        def emit_r2(bb):
            ent = [bb.tok("T"), bb.tok("V"), bb.tok("A")]
            bb.wait("sync", *ent)
            bb.wait("scalar", *ent)
            bb.wait("scalar", bb.tok("dG"))
            bb.wait("gpsimd", *ent)

            wl_tok = [None, None]
            for h in range(H):
                p = h % 2
                sl = slice(p * DC * 128, (p + 1) * DC * 128)
                dsl = slice(h * DC * 128, (h + 1) * DC * 128)
                tl = bb.dma("S", wland[:, sl], D["wq"][L, h, :, :],
                            waits=[wl_tok[p]])
                wl_tok[p] = bb.split(wqh[:, dsl], wql[:, dsl], wland[:, sl],
                                     waits=[tl])
            w_r2 = bb.tok("V")

            itok = {}
            gh = 0
            last_y = None
            prev_tSm = None
            prev_tqs = None
            prev_tspl = [None] * 4
            prev_gnb = None
            prev_tvst = None
            prev_wo_ev = [None, None]
            wg_cons = [None, None]
            wl_bias = [None, None]
            for qt in range(QT):
                pv = itok.get(gh - 1, []) + (ent if gh == 0 else [])
                tlh = bb.dma("A", l1h[:, :], D["sc_ln1"][qt, 0, :, :], waits=pv)
                tll = bb.dma("A", l1l[:, :], D["sc_ln1"][qt, 1, :, :])
                for h in range(H):
                    p = h % 2
                    pw = itok.get(gh - 2, [])
                    bb.wait("tensor", *pw)
                    bb.wait("vector", *pw)
                    psl = slice(p * 2 * QS, (p + 1) * 2 * QS)
                    ttb = bb.dma("A", ttab[:, psl],
                                 D["xtab"][h, qt, :, 0:2 * QS], waits=pw)
                    bb.dma("A", kfh[:, Q(p)], D["sc_kv"][h, qt, 0, :, :])
                    bb.dma("A", kfl[:, Q(p)], D["sc_kv"][h, qt, 1, :, :])
                    bb.dma("A", kth[:, Q(p)], D["sc_kv"][h, qt, 2, :, :])
                    bb.dma("A", ktl[:, Q(p)], D["sc_kv"][h, qt, 3, :, :])
                    bb.dma("A", vth[:, Q(p)], D["sc_kv"][h, qt, 4, :, :])
                    tk5 = bb.dma("A", vtl[:, Q(p)], D["sc_kv"][h, qt, 5, :, :])
                    TAB = lambda i: ttab[:, p * 2 * QS + i * QS:
                                         p * 2 * QS + (i + 1) * QS]
                    PP = lambda buf: buf[:, Q(p)]
                    KS = lambda buf, cci: buf[:, p * QS + cci * 128:
                                              p * QS + (cci + 1) * 128]

                    tQ = None
                    for dc in range(DC):
                        hh = slice(h * DC * 128 + dc * 128,
                                   h * DC * 128 + (dc + 1) * 128)
                        tQ = mm3(bb, pb[2][:, :], (wqh[:, hh], wql[:, hh]),
                                 (l1h[:, Q(dc)], l1l[:, Q(dc)]),
                                 start=(dc == 0), stop=(dc == DC - 1),
                                 waits=[w_r2, tlh, tll, prev_tqs]
                                 if dc == 0 else [], sig=(dc == DC - 1))
                    bb.v("tensor_tensor", PP(scA), pb[2][:, :], TAB(0), ALU.mult,
                         waits=[tQ, ttb])
                    bb.v("tensor_copy", PP(scB), pb[2][:, :])
                    bb.v("stream_shuffle", lnf[:, :], PP(scB), SHUF)
                    bb.v("tensor_tensor", lnf[:, :], lnf[:, :], TAB(1), ALU.mult)
                    txp = bb.v("tensor_tensor", PP(scA), PP(scA), lnf[:, :],
                               ALU.add, sig=True)
                    tqs = bb.split(PP(qph), PP(qpl), PP(scA), waits=[txp])
                    prev_tqs = tqs

                    for cci in range(4):
                        if cci == 0:
                            bb.split(s_hi[:, :], s_lo[:, :], s_slab[:, C1(h)],
                                     waits=[last_y, tinit, prev_tSm])
                        bb.mm(pb[4][:, C1(cci)], KS(kfh, cci), KS(qph, cci),
                              start=True, stop=False,
                              waits=[tk5, tqs, prev_tspl[cci]]
                              if cci == 0 else [prev_tspl[cci]])
                        bb.mm(pb[4][:, C1(cci)], KS(kfh, cci), KS(qpl, cci),
                              start=False, stop=False)
                        tsc = bb.mm(pb[4][:, C1(cci)], KS(kfl, cci),
                                    KS(qph, cci), start=False, stop=True,
                                    sig=True)
                        tmk = bb.v("tensor_tensor",
                                   scB[:, p * QS:p * QS + 128],
                                   pb[4][:, C1(cci)], MASK, ALU.mult,
                                   waits=[tsc], sig=True)
                        tspl = bb.split(sch[:, C1(cci)], scl[:, C1(cci)],
                                        scB[:, p * QS:p * QS + 128],
                                        waits=[tmk])
                        prev_tspl[cci] = tspl
                        bb.mm(pb[5][:, C1(cci)], KS(vth, cci), sch[:, C1(cci)],
                              start=True, stop=False, waits=[tspl])
                        bb.mm(pb[5][:, C1(cci)], KS(vth, cci), scl[:, C1(cci)],
                              start=False, stop=False)
                        bb.mm(pb[5][:, C1(cci)], KS(vtl, cci), sch[:, C1(cci)],
                              start=False, stop=False)
                        bb.mm(pb[5][:, C1(cci)], s_hi[:, :], KS(qph, cci),
                              start=False, stop=False)
                        bb.mm(pb[5][:, C1(cci)], s_hi[:, :], KS(qpl, cci),
                              start=False, stop=False)
                        last_y = bb.mm(pb[5][:, C1(cci)], s_lo[:, :],
                                       KS(qph, cci), start=False, stop=True,
                                       sig=True)
                        bb.mm(pb[6][:, 0:128], KS(kth, cci), KS(vth, cci),
                              start=True, stop=False)
                        bb.mm(pb[6][:, 0:128], KS(ktl, cci), KS(vth, cci),
                              start=False, stop=False)
                        tS = bb.mm(pb[6][:, 0:128], KS(kth, cci), KS(vtl, cci),
                                   start=False, stop=True, sig=True)
                        bb.v("tensor_tensor", s_slab[:, C1(h)],
                             s_slab[:, C1(h)], pb[6][:, 0:128], ALU.add,
                             waits=[tS])
                        tSm = bb.v("tensor_tensor", s_slab[:, C1(h)],
                                   s_slab[:, C1(h)], GAM(h), ALU.mult,
                                   sig=True)
                        if cci < 3:
                            bb.split(s_hi[:, :], s_lo[:, :], s_slab[:, C1(h)],
                                     waits=[last_y, tSm])
                    prev_tSm = tSm
                    tyf = bb.v("tensor_copy", PP(scA), pb[5][:, :],
                               waits=[last_y], sig=True)
                    bb.mm(pb[0][:, :], O128, PP(scA), start=True, stop=True,
                          waits=[tyf, prev_gnb] + prev_wo_ev)
                    ty2 = bb.a("activation", PP(scB), PP(scA), AF.Square,
                               waits=[tyf], sig=True)
                    tm2 = bb.mm(pb[1][:, :], O128, PP(scB), start=True,
                                stop=True, waits=[ty2, prev_tvst], sig=True)
                    bb.v("tensor_copy", mu[:, :], pb[0][:, :], waits=[tm2])
                    prev_gnb = bb.v("tensor_tensor",
                                    ymu[:, h * QS:(h + 1) * QS], PP(scA),
                                    mu[:, :], ALU.subtract, sig=True)
                    bb.v("tensor_tensor", msq[:, :], mu[:, :], mu[:, :],
                         ALU.mult)
                    prev_tvst = bb.v("tensor_tensor",
                                     vst[:, h * QS:(h + 1) * QS],
                                     pb[1][:, :], msq[:, :], ALU.subtract,
                                     sig=True)
                    itok[gh] = bb.snapc()
                    gh += 1
                tv1 = bb.tok("V")
                bb.a("activation", vst[:, :], vst[:, :], AF.Ln,
                     bias=G["EPSA"], waits=[tv1])
                trstd = bb.a("activation", vst[:, :], vst[:, :], AF.Exp,
                             scale=G["NHA"], sig=True)
                tgp = None
                et3 = {}
                for h in range(H):
                    p = h % 2
                    wsl = slice(p * DC * 128, (p + 1) * DC * 128)
                    tl = bb.dma("S", wland[:, wsl], D["wg"][L, h, :, :],
                                waits=[wl_tok[p], wl_bias[p]])
                    wl_tok[p] = tgs = bb.split(wgh[:, wsl], wgl[:, wsl],
                                               wland[:, wsl],
                                               waits=[tl, wg_cons[p]])
                    tG = None
                    for dc in range(DC):
                        gsl = slice(p * DC * 128 + dc * 128,
                                    p * DC * 128 + (dc + 1) * 128)
                        tG = mm3(bb, pb[2][:, :], (wgh[:, gsl], wgl[:, gsl]),
                                 (l1h[:, Q(dc)], l1l[:, Q(dc)]),
                                 start=(dc == 0), stop=(dc == DC - 1),
                                 waits=[tgp if tgp else trstd, tgs]
                                 if dc == 0 else [], sig=(dc == DC - 1))
                    wg_cons[p] = tG
                    tsil = bb.a("activation", scA[:, Q(p)], pb[2][:, :], AF.Silu,
                                waits=[tG, et3.get(h - 2)], sig=True)
                    tgp = tsil
                    hs = slice(h * QS, (h + 1) * QS)
                    bb.v("tensor_tensor", scB[:, Q(0)], ymu[:, hs], vst[:, hs],
                         ALU.mult, waits=[trstd, tsil])
                    tgm = bb.v("tensor_tensor", scB[:, Q(0)], scB[:, Q(0)],
                               scA[:, Q(p)], ALU.mult, sig=True)
                    et3[h] = bb.split(gyh[:, hs], gyl[:, hs], scB[:, Q(0)],
                                      waits=[tgm])
                tgy = bb.tok("V")
                prev_two = None
                for oc in range(DC):
                    p = oc % 2
                    wsl = slice(p * DC * 128, (p + 1) * DC * 128)
                    tl = bb.dma("S", wland[:, wsl], D["wo"][L, oc, :, :],
                                waits=[wl_tok[p], wl_bias[p]])
                    if qt == 0:
                        tb = None
                        for vc in range(DC):
                            tb = bb.mm(pb[3][:, oc:oc + 1],
                                       wland[:, p * DC * 128 + vc * 128:
                                             p * DC * 128 + (vc + 1) * 128],
                                       P1("gnb", L * DC + vc),
                                       start=(vc == 0), stop=(vc == DC - 1),
                                       waits=[tl, tgy] if vc == 0 else [],
                                       sig=(vc == DC - 1))
                        wl_bias[p] = tb
                        bb.v("tensor_copy", G["bias_wo"][:, oc:oc + 1],
                             pb[3][:, oc:oc + 1], waits=[tb])
                    tsw = None
                    for vc in range(DC):
                        tsw = bb.v("tensor_scalar", wos[:, C1(vc)],
                                   wland[:, p * DC * 128 + vc * 128:
                                         p * DC * 128 + (vc + 1) * 128],
                                   P1("gnw", L * DC + vc), None, ALU.mult,
                                   waits=[tl, prev_two] if vc == 0 else [],
                                   sig=(vc == DC - 1))
                    tws = bb.split(wohi[:, :], wolo[:, :], wos[:, :],
                                   waits=[tsw])
                    wl_tok[p] = tws
                    pbank = pb[0] if oc % 2 == 0 else pb[1]
                    txr = bb.dma("A", xt2[:, Q(p)], lay_src[L][qt, :, Q(oc)],
                                 waits=[prev_wo_ev[p]])
                    two = None
                    for vc in range(DC):
                        two = mm3(bb, pbank[:, :],
                                  (wohi[:, C1(vc)], wolo[:, C1(vc)]),
                                  (gyh[:, vc * QS:(vc + 1) * QS],
                                   gyl[:, vc * QS:(vc + 1) * QS]),
                                  start=(vc == 0), stop=(vc == DC - 1),
                                  waits=[tws, tgy, prev_wo_ev[p]]
                                  if vc == 0 else [], sig=(vc == DC - 1))
                    prev_two = two
                    bb.v("tensor_scalar", woev[:, :], pbank[:, :],
                         G["bias_wo"][:, oc:oc + 1], None, ALU.add,
                         waits=[two, txr])
                    tev = bb.v("tensor_tensor", xt2[:, Q(p)], woev[:, :],
                               xt2[:, Q(p)], ALU.add, sig=True)
                    prev_wo_ev[p] = bb.dma("G", D["sc_y"][qt, :, Q(oc)],
                                           xt2[:, Q(p)], waits=[tev])
                itok[gh - 1] = bb.snapc()

        bb.run_phase(emit_r2)

    if stop == f"R2L{L}":
        return
    with contextlib.ExitStack() as S3:
        sb = lambda n, s, d=F32: S3.enter_context(
            nc.sbuf_tensor(n + "ffn" + str(L), s, d))
        yt = sb("yt", [128, DC * QS])
        l2h = sb("l2h", [128, DC * QS], BF16)
        l2l = sb("l2l", [128, DC * QS], BF16)
        w1h = sb("w1h", [128, 2 * DC * 128], BF16)
        w1l = sb("w1l", [128, 2 * DC * 128], BF16)
        w2h = sb("w2h", [128, FC * 128], BF16)
        w2l = sb("w2l", [128, FC * 128], BF16)
        hh_ = sb("hh", [128, FC * QS], BF16)
        hl_ = sb("hl", [128, FC * QS], BF16)
        gsc = sb("gsc", [128, 2 * QS])
        esc = sb("esc", [128, QS])

        def emit_ffn(bb):
            ent = [bb.tok("T"), bb.tok("V"), bb.tok("A")]
            bb.wait("sync", *ent)
            bb.wait("sync", bb.tok("dG"))
            bb.wait("scalar", *ent)
            bb.wait("scalar", bb.tok("dG"))
            bb.wait("gpsimd", *ent)

            ln_stats = G["ln_stats"]
            prev = []
            for qt in range(QT):
                ty = bb.dma("A", yt[:, :], D["sc_y"][qt, :, :],
                            waits=prev + (ent if qt == 0 else []))
                trs = ln_stats(yt, waits=[ty])
                for dc in range(DC):
                    bb.v("tensor_tensor", gsc[:, Q(0)], yt[:, Q(dc)], mu[:, :],
                         ALU.subtract, waits=[trs] if dc == 0 else [])
                    bb.v("tensor_tensor", gsc[:, Q(0)], gsc[:, Q(0)], rstd[:, :],
                         ALU.mult)
                    tln = bb.v("tensor_scalar", lnf[:, :], gsc[:, Q(0)],
                               P1("ln2w", L * DC + dc),
                               P1("ln2b", L * DC + dc),
                               ALU.mult, ALU.add, sig=True)
                    bb.split(l2h[:, Q(dc)], l2l[:, Q(dc)], lnf[:, :],
                             waits=[tln])
                tl2 = bb.tok("V")
                wtok = [None, None]
                etok = {}
                for oc in range(FC):
                    p = oc % 2
                    wsl = slice(p * DC * 128, (p + 1) * DC * 128)
                    bb.dma("S", w1h[:, wsl], D["sc_w1"][L, oc, 0, :, :],
                           waits=[wtok[p]])
                    twl = bb.dma("S", w1l[:, wsl], D["sc_w1"][L, oc, 1, :, :])
                    pbank = pb[2 + (oc % 4)]
                    tmm = None
                    for dc in range(DC):
                        tmm = mm3(bb, pbank[:, :],
                                  (w1h[:, p * DC * 128 + dc * 128:
                                       p * DC * 128 + (dc + 1) * 128],
                                   w1l[:, p * DC * 128 + dc * 128:
                                       p * DC * 128 + (dc + 1) * 128]),
                                  (l2h[:, Q(dc)], l2l[:, Q(dc)]),
                                  start=(dc == 0), stop=(dc == DC - 1),
                                  waits=[twl, tl2, etok.get(oc - 4)]
                                  if dc == 0 else [], sig=(dc == DC - 1))
                    wtok[p] = tmm
                    tg = bb.a("activation", gsc[:, Q(oc % 2)], pbank[:, :],
                              AF.Gelu, bias=P1("fb1", L * FC + oc),
                              waits=[tmm, etok.get(oc - 2)], sig=True)
                    etok[oc] = bb.split(hh_[:, Q(oc)], hl_[:, Q(oc)],
                                        gsc[:, Q(oc % 2)], waits=[tg])
                th_all = bb.tok("V")
                w2tok = None
                etok2 = {}
                for oc in range(DC):
                    bb.dma("S", w2h[:, :], D["sc_w2"][L, oc, 0, :, :],
                           waits=[w2tok])
                    twl = bb.dma("S", w2l[:, :], D["sc_w2"][L, oc, 1, :, :])
                    pbank = pb[2 + (oc % 4)]
                    tmm = None
                    for fc in range(FC):
                        tmm = mm3(bb, pbank[:, :],
                                  (w2h[:, C1(fc)], w2l[:, C1(fc)]),
                                  (hh_[:, Q(fc)], hl_[:, Q(fc)]),
                                  start=(fc == 0), stop=(fc == FC - 1),
                                  waits=[twl, th_all, etok2.get(oc - 4)]
                                  if fc == 0 else [], sig=(fc == FC - 1))
                    w2tok = tmm
                    bb.v("tensor_scalar", esc[:, :], pbank[:, :],
                         P1("fb2", L * DC + oc), None, ALU.add, waits=[tmm])
                    etok2[oc] = bb.v("tensor_tensor", yt[:, Q(oc)], esc[:, :],
                                     yt[:, Q(oc)], ALU.add, sig=True)
                    bb.dma("G", D["sc_x"][qt, :, Q(oc)], yt[:, Q(oc)],
                           waits=[etok2[oc]])
                prev = bb.snapc()

        bb.run_phase(emit_ffn)


def tail(nc, bb, D, G):
    pb = G["pb"]
    mu, rstd = G["mu"], G["rstd"]
    ln_stats = G["ln_stats"]
    P1 = G["P1"]

    with contextlib.ExitStack() as S4:
        sb = lambda n, s, d=F32: S4.enter_context(nc.sbuf_tensor(n, s, d))
        xt = sb("xtt", [128, DC * QS])
        xta = sb("xta", [128, DC * QS])
        osc = sb("osc", [128, QS])
        def emit_topln(bb):
            ent = [bb.tok("T"), bb.tok("V"), bb.tok("A")]
            bb.wait("scalar", *ent)
            bb.wait("scalar", bb.tok("dG"))
            bb.wait("gpsimd", *ent)
            prev = []
            for qt in range(QT):
                t1 = bb.dma("A", xt[:, :], D["sc_x"][qt, :, :],
                            waits=prev + (ent if qt == 0 else []))
                t2 = bb.dma("A", xta[:, :], D["x0"][qt, :, :])
                tadd = None
                for dc in range(DC):
                    tadd = bb.v("tensor_tensor", xt[:, Q(dc)], xt[:, Q(dc)],
                                xta[:, Q(dc)], ALU.add,
                                waits=[t1, t2] if dc == 0 else [], sig=True)
                trs = ln_stats(xt, waits=[tadd])
                td = None
                for dc in range(DC):
                    bb.v("tensor_tensor", osc[:, :], xt[:, Q(dc)], mu[:, :],
                         ALU.subtract,
                         waits=[trs, td] if dc == 0 else [td])
                    tn = bb.v("tensor_tensor", osc[:, :], osc[:, :], rstd[:, :],
                              ALU.mult, sig=True)
                    td = bb.dma("G", D["sc_x1"][qt, :, Q(dc)], osc[:, :],
                                waits=[tn])
                prev = bb.snapc()

        bb.run_phase(emit_topln)

    with contextlib.ExitStack() as S5:
        sb = lambda n, s, d=F32: S5.enter_context(nc.sbuf_tensor(n, s, d))
        x1t = sb("x1t", [128, DC * QS])
        thh = sb("thh", [128, FC * QS], BF16)
        thl = sb("thl", [128, FC * QS], BF16)
        t1h = sb("t1h", [128, FC * QS], BF16)
        t1l = sb("t1l", [128, FC * QS], BF16)
        x1h = t1h
        x1l = t1l
        wa = sb("wa", [128, 2 * DC * 128], BF16)
        wb = sb("wb", [128, 2 * DC * 128], BF16)
        wbig = sb("wbig", [128, 2 * FC * 128], BF16)
        sA = sb("sA", [128, QS])
        sB = sb("sB", [128, QS])
        def emit_tern(bb):
            ent = [bb.tok("T"), bb.tok("V"), bb.tok("A")]
            bb.wait("scalar", *ent)
            bb.wait("scalar", bb.tok("dG"))
            bb.wait("gpsimd", *ent)
            bb.wait("sync", *ent)
            prev = []
            wtok_ab = [None, None]
            wtok_big = [None, None]
            for qt in range(QT):
                tx = bb.dma("A", x1t[:, :], D["sc_x1"][qt, :, :],
                            waits=prev + (ent if qt == 0 else []))
                for dc in range(DC):
                    bb.split(x1h[:, Q(dc)], x1l[:, Q(dc)], x1t[:, Q(dc)],
                             waits=[tx])
                tx1s = bb.tok("V")
                etok = {}
                for oc in range(FC):
                    p = oc % 2
                    wsl = slice(p * DC * 128, (p + 1) * DC * 128)
                    bb.dma("G", wa[:, wsl], D["tw1"][oc, :, :],
                           waits=[wtok_ab[p]])
                    twl = bb.dma("G", wb[:, wsl], D["tw2"][oc, :, :])
                    pa = pb[2 + (oc % 2) * 2]
                    pk = pb[3 + (oc % 2) * 2]
                    for dc in range(DC):
                        ws1 = slice(p * DC * 128 + dc * 128,
                                    p * DC * 128 + (dc + 1) * 128)
                        bb.mm(pa[:, :], wa[:, ws1], x1h[:, Q(dc)],
                              start=(dc == 0), stop=False,
                              waits=[twl, tx1s, etok.get(oc - 2)]
                              if dc == 0 else [])
                        bb.mm(pa[:, :], wa[:, ws1], x1l[:, Q(dc)],
                              start=False, stop=(dc == DC - 1))
                    tmm = None
                    for dc in range(DC):
                        ws1 = slice(p * DC * 128 + dc * 128,
                                    p * DC * 128 + (dc + 1) * 128)
                        bb.mm(pk[:, :], wb[:, ws1], x1h[:, Q(dc)],
                              start=(dc == 0), stop=False)
                        tmm = bb.mm(pk[:, :], wb[:, ws1], x1l[:, Q(dc)],
                                    start=False, stop=(dc == DC - 1),
                                    sig=(dc == DC - 1))
                    wtok_ab[p] = tmm
                    ta = bb.a("activation", sA[:, :], pa[:, :], AF.Silu,
                              scale=P1("w1s", oc),
                              waits=[tmm, etok.get(oc - 1)], sig=True)
                    bb.v("tensor_scalar", sB[:, :], pk[:, :], P1("w2s", oc),
                         None, ALU.mult, waits=[tmm])
                    thm = bb.v("tensor_tensor", sB[:, :], sA[:, :],
                               sB[:, :], ALU.mult, waits=[ta], sig=True)
                    etok[oc] = bb.split(thh[:, Q(oc)], thl[:, Q(oc)],
                                        sB[:, :], waits=[thm])
                tT1 = bb.tok("V")
                etok = {}
                for oc in range(FC):
                    p = oc % 2
                    wsl = slice(p * FC * 128, (p + 1) * FC * 128)
                    twl = bb.dma("G", wbig[:, wsl], D["th1"][oc, :, :],
                                 waits=[wtok_big[p]])
                    pbank = pb[2 + (oc % 4)]
                    tmm = None
                    for ic in range(FC):
                        ws1 = slice(p * FC * 128 + ic * 128,
                                    p * FC * 128 + (ic + 1) * 128)
                        bb.mm(pbank[:, :], wbig[:, ws1], thh[:, Q(ic)],
                              start=(ic == 0), stop=False,
                              waits=[twl, tT1, etok.get(oc - 4)]
                              if ic == 0 else [])
                        tmm = bb.mm(pbank[:, :], wbig[:, ws1], thl[:, Q(ic)],
                                    start=False, stop=(ic == FC - 1),
                                    sig=(ic == FC - 1))
                    wtok_big[p] = tmm
                    tg = bb.a("activation", sA[:, :], pbank[:, :], AF.Gelu,
                              scale=P1("h1s", oc),
                              waits=[tmm, etok.get(oc - 1)], sig=True)
                    etok[oc] = bb.split(t1h[:, Q(oc)], t1l[:, Q(oc)], sA[:, :],
                                        waits=[tg])
                tT2 = bb.tok("V")
                etok = {}
                for oc in range(FC):
                    p = oc % 2
                    wsl = slice(p * FC * 128, (p + 1) * FC * 128)
                    twl = bb.dma("G", wbig[:, wsl], D["th2"][oc, :, :],
                                 waits=[wtok_big[p]])
                    pbank = pb[2 + (oc % 4)]
                    tmm = None
                    for ic in range(FC):
                        ws1 = slice(p * FC * 128 + ic * 128,
                                    p * FC * 128 + (ic + 1) * 128)
                        bb.mm(pbank[:, :], wbig[:, ws1], t1h[:, Q(ic)],
                              start=(ic == 0), stop=False,
                              waits=[twl, tT2, etok.get(oc - 4)]
                              if ic == 0 else [])
                        tmm = bb.mm(pbank[:, :], wbig[:, ws1], t1l[:, Q(ic)],
                                    start=False, stop=(ic == FC - 1),
                                    sig=(ic == FC - 1))
                    wtok_big[p] = tmm
                    tg = bb.a("activation", sA[:, :], pbank[:, :], AF.Gelu,
                              scale=P1("h2s", oc),
                              waits=[tmm, etok.get(oc - 1)], sig=True)
                    etok[oc] = bb.split(thh[:, Q(oc)], thl[:, Q(oc)], sA[:, :],
                                        waits=[tg])
                tT3 = bb.tok("V")
                tprev = None
                for oc in range(DC):
                    p = oc % 2
                    wsl = slice(p * FC * 128, (p + 1) * FC * 128)
                    twl = bb.dma("G", wbig[:, wsl], D["tw3"][oc, :, :],
                                 waits=[wtok_big[p]])
                    tmm = None
                    for ic in range(FC):
                        ws1 = slice(p * FC * 128 + ic * 128,
                                    p * FC * 128 + (ic + 1) * 128)
                        bb.mm(pb[6][:, :], wbig[:, ws1], thh[:, Q(ic)],
                              start=(ic == 0), stop=False,
                              waits=[twl, tT3, tprev] if ic == 0 else [])
                        tmm = bb.mm(pb[6][:, :], wbig[:, ws1], thl[:, Q(ic)],
                                    start=False, stop=(ic == FC - 1),
                                    sig=(ic == FC - 1))
                    wtok_big[p] = tmm
                    bb.v("tensor_scalar", sA[:, :], pb[6][:, :], P1("w3s", oc),
                         None, ALU.mult, waits=[tmm])
                    tprev = bb.v("tensor_tensor", x1t[:, Q(oc)], sA[:, :],
                                 x1t[:, Q(oc)], ALU.add, sig=True)
                trs = ln_stats(x1t, waits=[tprev])
                td = None
                for dc in range(DC):
                    bb.v("tensor_tensor", sA[:, :], x1t[:, Q(dc)], mu[:, :],
                         ALU.subtract,
                         waits=[trs, td] if dc == 0 else [td])
                    tn = bb.v("tensor_tensor", sA[:, :], sA[:, :], rstd[:, :],
                              ALU.mult, sig=True)
                    td = bb.dma("S", D["out"][qt, :, Q(dc)], sA[:, :],
                                waits=[tn])
                prev = bb.snapc()

        bb.run_phase(emit_tern)


_GAMMAS = 1.0 - np.exp(np.linspace(np.log(1 / 32), np.log(1 / 512), H))


def host_tables(parity):
    g = np.arange(parity * T, parity * T + T, dtype=np.float64)
    base = (np.arange(0, 128, 2, dtype=np.float64) + 0.4 * 128) / (1.4 * 128)
    sc = base[None, :] ** (g / 512.0)[:, None]
    inv = 1.0 / 10000.0 ** (np.arange(64, dtype=np.float64) / 64)
    ang = g[:, None] * inv[None, :]
    sin, cos = np.sin(ang), np.cos(ang)
    loc = (np.arange(T) % CH).astype(np.float64)
    dupi = np.arange(128) // 2
    sgn = np.where(np.arange(128) % 2 == 0, -1.0, 1.0)
    tabs = np.zeros((H, 4, 128, T), dtype=np.float64)
    for h in range(H):
        gq = _GAMMAS[h] ** loc
        gk = _GAMMAS[h] ** (-loc)
        tabs[h, 0] = (cos * sc)[:, dupi].T * gq[None, :]
        tabs[h, 1] = (sin * sc)[:, dupi].T * gq[None, :] * sgn[:, None]
        tabs[h, 2] = (cos / sc)[:, dupi].T * gk[None, :]
        tabs[h, 3] = (sin / sc)[:, dupi].T * gk[None, :] * sgn[:, None]
    t4 = tabs.reshape(H, 4, 128, QT, QS).transpose(0, 3, 2, 1, 4)
    return np.ascontiguousarray(
        t4.reshape(H, QT, 128, 4 * QS).astype(np.float32))


def host_consts():
    cons = np.zeros((128, NCON), dtype=np.float32)
    m = (np.arange(128)[None, :] >= np.arange(128)[:, None])
    cons[:, CON_MASK:CON_MASK + 128] = m.astype(np.float32)
    cons[:, CON_O128:CON_O128 + 128] = 1.0 / 128
    cons[:, CON_O1024:CON_O1024 + 128] = 1.0 / 1024
    for h in range(H):
        cons[:, CON_GAM + h * 128:CON_GAM + (h + 1) * 128] = np.float32(
            _GAMMAS[h] ** 128)
    cons[:, CON_EPS] = EPS
    cons[:, CON_NH] = -0.5
    return cons


def host_pars(I):
    p = np.zeros((128, NPAR), dtype=np.float32)

    def put(nm, arr):
        a = np.asarray(arr)
        p[:, _PAR[nm]:_PAR[nm] + a.shape[1]] = a

    put("ln1w", I["L0_attn_ln1_w"].reshape(2, 8, 128).transpose(2, 0, 1)
        .reshape(128, 16))
    put("ln1b", I["L0_attn_ln1_b"].reshape(2, 8, 128).transpose(2, 0, 1)
        .reshape(128, 16))
    put("ln2w", I["L0_attn_ln2_w"].reshape(2, 8, 128).transpose(2, 0, 1)
        .reshape(128, 16))
    put("ln2b", I["L0_attn_ln2_b"].reshape(2, 8, 128).transpose(2, 0, 1)
        .reshape(128, 16))
    put("gnw", I["L0_attn_gn_w"].reshape(2, 8, 128).transpose(2, 0, 1)
        .reshape(128, 16))
    put("gnb", I["L0_attn_gn_b"].reshape(2, 8, 128).transpose(2, 0, 1)
        .reshape(128, 16))
    put("fb1", I["L0_attn_ffn_b1"].reshape(2, 32, 128).transpose(2, 0, 1)
        .reshape(128, 64))
    put("fb2", I["L0_attn_ffn_b2"].reshape(2, 8, 128).transpose(2, 0, 1)
        .reshape(128, 16))
    put("w1s", I["L0_ffn_w1_s"].reshape(32, 128).T)
    put("w2s", I["L0_ffn_w2_s"].reshape(32, 128).T)
    put("h1s", I["L0_ffn_h1_s"].reshape(32, 128).T)
    put("h2s", I["L0_ffn_h2_s"].reshape(32, 128).T)
    put("w3s", I["L0_ffn_w3_s"].reshape(8, 128).T)
    return p


def host_weights(I):
    w = {}

    def c(a):
        return np.ascontiguousarray(a, dtype=np.float32)

    for nm, key in (("wq", "L0_attn_Wq"), ("wk", "L0_attn_Wk"),
                    ("wv", "L0_attn_Wv")):
        a = I[key].reshape(2, H, DC, 128, 128).transpose(0, 1, 3, 2, 4)
        w[nm] = c(a.reshape(2, H, 128, DC * 128))
    a = I["L0_attn_W_G"].reshape(2, DC, 128, H, 128).transpose(0, 3, 2, 1, 4)
    w["wg"] = c(a.reshape(2, H, 128, DC * 128))
    a = I["L0_attn_W_O"].reshape(2, DC, 128, DC, 128).transpose(0, 3, 2, 1, 4)
    w["wo"] = c(a.reshape(2, DC, 128, DC * 128))
    a = I["L0_attn_ffn_w1"].reshape(2, DC, 128, FC, 128).transpose(
        0, 3, 2, 1, 4)
    w["fw1"] = c(a.reshape(2, FC, 128, DC * 128))
    a = I["L0_attn_ffn_w2"].reshape(2, FC, 128, DC, 128).transpose(
        0, 3, 2, 1, 4)
    w["fw2"] = c(a.reshape(2, DC, 128, FC * 128))
    for nm, key, n_in, n_out in (("tw1", "L0_ffn_w1_w", DC, FC),
                                 ("tw2", "L0_ffn_w2_w", DC, FC),
                                 ("th1", "L0_ffn_h1_w", FC, FC),
                                 ("th2", "L0_ffn_h2_w", FC, FC),
                                 ("tw3", "L0_ffn_w3_w", FC, DC)):
        a = np.ascontiguousarray(I[key].T)
        a = a.reshape(n_in, 128, n_out, 128).transpose(2, 1, 0, 3)
        w[nm] = c(a.reshape(n_out, 128, n_in * 128))
    return w


_CACHE = {}


def kernel(**inputs):
    I = {k: np.asarray(v) for k, v in inputs.items()}
    if "nc" not in _CACHE:
        _CACHE["nc"] = build()
    nc = _CACHE["nc"]

    w = host_weights(I)
    pars = host_pars(I)
    cons = host_consts()
    consb = np.eye(128, dtype=BF)
    tabs = [host_tables(0), host_tables(1)]
    X = I["L0_attn_X"]

    in_maps = []
    for c in range(8):
        s, par = c // 2, c % 2
        xs = np.ascontiguousarray(X[s, par * T:(par + 1) * T, :].T,
                                  dtype=np.float32)
        x0 = xs.reshape(DC, 128, QT, QS).transpose(2, 1, 0, 3)
        x0 = np.ascontiguousarray(x0.reshape(QT, 128, DC * QS))
        m = dict(w)
        m.update(x0=x0, xtab=tabs[par], pars=pars, cons=cons, consb=consb,
                 maska=np.full((128, 1), float(par), np.float32))
        in_maps.append(m)

    res = run_bass_kernel_spmd(nc, in_maps, core_ids=list(range(8)))
    out = np.empty((4, 4096, 1024), dtype=np.float32)
    for c in range(8):
        s, par = c // 2, c % 2
        o = np.asarray(res.results[c]["out"])
        o = o.reshape(QT, 128, DC, QS).transpose(2, 1, 0, 3)
        out[s, par * T:(par + 1) * T, :] = o.reshape(1024, T).T
    return out
